# version 20
# baseline (speedup 1.0000x reference)
import math
import numpy as np
from contextlib import ExitStack

import concourse.bass as bass
import concourse.mybir as mybir
from concourse.bass_utils import run_bass_kernel_spmd

F32 = mybir.dt.float32
BF16 = mybir.dt.bfloat16
AF = mybir.ActivationFunctionType
ALU = mybir.AluOpType
AXX = mybir.AxisListType.X

EPS = 1e-5
CSHIFT = 5.0


class Sem:
    def __init__(self, nc, stack, name):
        self.h = stack.enter_context(nc.semaphore(name))
        self.n = 0


def build_nc(T=2048, D=2048, NH=16, phases=5, dbg=False, p4lite=0):
    HD = D // NH
    assert HD == 128 and T % 256 == 0 and D % 256 == 0
    NT = T // 128
    ND = D // 128
    TH = T // 2
    DH = D // 2
    SC = 1.0 / math.sqrt(HD)
    PQ = 512

    nc = bass.Bass("TRN2", target_bir_lowering=False, detect_race_conditions=False)

    x_in = nc.dram_tensor("x", [T, D], F32, kind="ExternalInput")
    w_in = {k: nc.dram_tensor(k, [D, D], F32, kind="ExternalInput")
            for k in ("wq", "wk", "wv", "wo", "wf")}
    b_in = nc.dram_tensor("fbias", [1, D], F32, kind="ExternalInput")
    out = nc.dram_tensor("out", [T, D], F32, kind="ExternalOutput")

    KD = "ExternalOutput" if dbg else "Internal"
    s_w = {k: nc.dram_tensor("s_" + k, [D, D], BF16, kind=KD) for k in ("wq", "wk", "wv", "wo")}
    s_qf = nc.dram_tensor("s_qf", [D, D], BF16, kind=KD)
    s_xn1 = nc.dram_tensor("s_xn1", [T, D], BF16, kind=KD)
    s_qt = nc.dram_tensor("s_qt", [D, T], BF16, kind=KD)
    s_kt = nc.dram_tensor("s_kt", [D, T], BF16, kind=KD)
    s_v = nc.dram_tensor("s_v", [T, D], BF16, kind=KD)
    s_woT = nc.dram_tensor("s_woT", [D, D], BF16)
    s_qfT = nc.dram_tensor("s_qfT", [D, D], BF16)
    s_xn2T = nc.dram_tensor("s_xn2T", [D, T], BF16)
    s_x2 = nc.dram_tensor("s_x2", [T, D], F32, kind=KD)
    s_xn2 = nc.dram_tensor("s_xn2", [T, D], BF16, kind=KD)

    poff = []
    acc = 0
    for kc in range(NT):
        poff.append(acc)
        acc += T - kc * 128
    PTOT = acc

    def pieces(lo, hi, base):
        res = []
        p = lo
        while p < hi:
            q = min(hi, base + ((p - base) // PQ + 1) * PQ)
            res.append((p, q))
            p = q
        return res

    with ExitStack() as st0:
        en0 = st0.enter_context
        ones_sb = en0(nc.sbuf_tensor("ones_sb", [128, 128], BF16))
        masktile = en0(nc.sbuf_tensor("masktile", [128, 128], BF16))
        ident = en0(nc.sbuf_tensor("ident", [128, 128], BF16))

        sems = {}
        for name in [
            "p1xl0", "p1xl1", "p1xl2", "p1xl3", "p1sq", "p1var", "p1ap", "p1st", "p1tt", "p1tx",
            "gcast", "p2tr", "p2mm", "p2ev", "p2st",
            "p3hd0", "p3hd1", "p3hv0", "p3hv1", "p3hv2", "p3hv3", "p3s", "p3e", "p3m", "p3l", "p3rc", "p3z", "p3zev",
            "wf0", "wf1", "tq0", "tq1", "terv",
            "p3ws", "p3wt", "p4qs", "p4qt", "p4xs", "p4xt", "p5xs", "p5xt", "p4tr", "p4mm", "p4sq", "p4var", "p4rs", "p4ap", "p4x2", "p4xn",
            "p4xl0", "p4xl1", "p4s2", "p4sn",
            "p5tr", "p5trs", "p5mm", "p5yb", "p5sq", "p5var", "p5rs", "p5ap", "p5out",
            "p5xl0", "p5xl1", "biasld", "vch", "sch", "red",
        ]:
            sems[name] = Sem(nc, st0, "sem_" + name)

        def run_phase(prog):
            with nc.Block() as blk:
                def mk(fns):
                    def body(e):
                        for f in fns:
                            f(e)
                    return body
                blk.sync(mk(prog["sync"]))
                blk.gpsimd(mk(prog["gpsimd"]))
                blk.vector(mk(prog["vector"]))
                blk.scalar(mk(prog["scalar"]))
                blk.tensor(mk(prog["tensor"]))

        def new_prog():
            return {k: [] for k in ("sync", "gpsimd", "vector", "scalar", "tensor")}

        def do(prog, eng, build, inc=None, inc_k=1, waits=(), sink=None):
            w = [(s.h, v) for (s, v) in waits if v > 0]
            ih = None
            if inc is not None:
                inc.n += inc_k
                ih = inc.h
            ik = inc_k

            def f(e, w=w, build=build, ih=ih, ik=ik):
                for (h, v) in w:
                    e.wait_ge(h, v)
                inst = build(e)
                if ih is not None:
                    inst.then_inc(ih, ik)
            (prog[eng] if sink is None else sink).append(f)

        def waitonly(prog, eng, waits):
            w = [(s.h, v) for (s, v) in waits if v > 0]

            def f(e, w=w):
                for (h, v) in w:
                    e.wait_ge(h, v)
            prog[eng].append(f)


        def drain_all(extra=()):
            pd = new_prog()
            ws = [(s, s.n) for s in sems.values() if s.n > 0]
            waitonly(pd, "sync", [(s, v) for (s, v) in ws])
            run_phase(pd)

        st1 = ExitStack()
        en1 = st1.enter_context
        xn1T = en1(nc.sbuf_tensor("xn1T", [128, ND * T], BF16))
        HW0 = D // 2
        wS0 = en1(nc.sbuf_tensor("wS0", [128, ND * HW0], BF16))
        wfb = en1(nc.sbuf_tensor("wfb0", [128, D], F32))
        ter = en1(nc.sbuf_tensor("tter0", [128, D], BF16))
        qb = [en1(nc.sbuf_tensor(f"qb{i}", [128, D], BF16)) for i in range(2)]
        tstat = en1(nc.sbuf_tensor("tstat", [128, 4], F32))
        st1p = ExitStack()
        en1p = st1p.enter_context
        pst = en1p(nc.psum_tensor("pst", [128, 2048], BF16))
        xbuf = [en1p(nc.sbuf_tensor(f"p1x{i}", [128, D], F32)) for i in range(3)]
        xnb = [en1p(nc.sbuf_tensor(f"p1n{i}", [128, D], BF16)) for i in range(2)]
        scrap1 = en1p(nc.sbuf_tensor("p1scrap", [128, D], F32))
        stt1 = [en1p(nc.sbuf_tensor(f"p1s{i}", [128, 8], F32)) for i in range(2)]

        p = new_prog()
        do(p, "gpsimd", lambda e: e.memset(ones_sb[:, :], 1.0))
        do(p, "gpsimd", lambda e: e.memset(masktile[:, :], 1.0))
        do(p, "gpsimd", lambda e: e.affine_select(masktile[:, :], masktile[:, :],
                                                  [[1, 128]], ALU.is_ge, 0.0,
                                                  base=0, channel_multiplier=-1))
        do(p, "gpsimd", lambda e: e.memset(ident[:, :], 1.0))
        do(p, "gpsimd", lambda e: e.affine_select(ident[:, :], ident[:, :],
                                                  [[1, 128]], ALU.is_ge, 0.0,
                                                  base=0, channel_multiplier=-1))
        do(p, "gpsimd", lambda e: e.affine_select(ident[:, :], ident[:, :],
                                                  [[-1, 128]], ALU.is_ge, 0.0,
                                                  base=0, channel_multiplier=1))
        NC4 = max(1, (D * D * 4) // (4 << 20))
        CR = D // NC4
        gc_after = {}
        for wname in ("wq",):
            for i in range(NC4):
                sl = slice(i * CR, (i + 1) * CR)
                do(p, "gpsimd",
                   lambda e, wn=wname, sl=sl: e.dma_start(s_w[wn][sl, :], w_in[wn][sl, :]),
                   inc=sems["gcast"], inc_k=16)
            gc_after[wname] = sems["gcast"].n

        xl = [sems["p1xl0"], sems["p1xl1"], sems["p1xl2"]]

        def p1_load(tc):
            par = tc % 3
            lw = [(sems["p1ap"], tc - 2)] if tc >= 3 else []
            do(p, "sync",
               lambda e, tc=tc, par=par: e.dma_start(xbuf[par][:, :],
                                                     x_in[tc * 128:(tc + 1) * 128, :]),
               inc=xl[par], inc_k=16, waits=lw)

        def emit_ternary_fullrow():
            terv = sems["terv"]
            vch = sems["vch"]
            wfl = sems["wf0"]
            tqs = [sems["tq0"], sems["tq1"]]
            for oc in range(ND):
                par = oc % 2
                do(p, "gpsimd",
                   lambda e, oc=oc: e.dma_start(wfb[:, :], w_in["wf"][oc * 128:(oc + 1) * 128, :]),
                   inc=wfl, inc_k=16,
                   waits=[(terv, 3 * oc)] if oc >= 1 else [])
                do(p, "vector",
                   lambda e: e.tensor_reduce(
                       tstat[:, 0:1], wfb[:, :], AXX, ALU.add,
                       apply_absolute_value=True),
                   waits=[(wfl, 16 * (oc + 1))], inc=vch)
                do(p, "vector", lambda e: e.tensor_scalar(tstat[:, 1:2], tstat[:, 0:1], 0.7 / D, None, ALU.mult),
                   waits=[(vch, vch.n)], inc=vch)
                do(p, "vector", lambda e: e.tensor_scalar(tstat[:, 2:3], tstat[:, 1:2], -1.0, None, ALU.mult),
                   waits=[(vch, vch.n)], inc=vch)
                waitonly(p, "vector", [(vch, vch.n)])
                do(p, "vector",
                   lambda e: e.tensor_scalar(ter[:, :], wfb[:, :],
                                             tstat[:, 1:2], None, ALU.is_gt),
                   inc=terv)
                do(p, "vector",
                   lambda e, par=par: e.tensor_scalar(qb[par][:, :], wfb[:, :],
                                                      tstat[:, 2:3], None, ALU.is_lt),
                   waits=([(tqs[par], 16 * (oc // 2))] if oc >= 2 else []), inc=terv)
                do(p, "vector",
                   lambda e, par=par: e.tensor_tensor(qb[par][:, :], ter[:, :],
                                                      qb[par][:, :], ALU.subtract),
                   inc=terv)
                do(p, "gpsimd",
                   lambda e, oc=oc, par=par: e.dma_start(
                       s_qf[oc * 128:(oc + 1) * 128, :], qb[par][:, :]),
                   inc=tqs[par], inc_k=16, waits=[(sems["terv"], sems["terv"].n)])

        s0_emitted = [0]

        def slab0_tr(ic):
            first = (s0_emitted[0] == 0)
            s0_emitted[0] += 1
            do(p, "sync",
               lambda e, ic=ic: e.dma_start(
                   wS0[:, ic * HW0:(ic + 1) * HW0],
                   s_w["wq"][0:HW0, ic * 128:(ic + 1) * 128], transpose=True),
               inc=sems["p2tr"], inc_k=16,
               waits=[(sems["gcast"], gc_after["wq"])] if first else ())

        for tc in range(min(3, NT)):
            p1_load(tc)
        for tc in range(NT):
            par = tc % 2
            par4x = tc % 3
            s = stt1[par]
            ldv = 16 * (tc // 3 + 1)
            red, sch = sems["red"], sems["sch"]
            do(p, "vector", lambda e, par4x=par4x, s=s: e.reduce_sum(s[:, 0:1], xbuf[par4x][:, :], axis=AXX),
               waits=[(xl[par4x], ldv), (sems["p1ap"], tc - 1)], inc=red)
            do(p, "scalar", lambda e, par4x=par4x, s=s: e.activation(scrap1[:, :], xbuf[par4x][:, :], AF.Square,
                                                                accum_out=s[:, 1:2]),
               inc=sch, waits=[(xl[par4x], ldv)])
            sqv = sch.n
            do(p, "scalar", lambda e, s=s: e.mul(s[:, 2:3], s[:, 0:1], -1.0 / D),
               waits=[(red, red.n)])
            do(p, "scalar", lambda e, s=s: e.activation(s[:, 3:4], s[:, 2:3], AF.Square))
            do(p, "scalar", lambda e, s=s: e.mul(s[:, 4:5], s[:, 3:4], -1.0), inc=sch)
            negv = sch.n
            do(p, "scalar", lambda e, s=s: e.mul(s[:, 7:8], s[:, 1:2], 1.0 / D),
               waits=[(sch, sqv)])
            do(p, "scalar", lambda e, s=s: e.add(s[:, 7:8], s[:, 7:8], s[:, 4:5]),
               waits=[(sch, negv)])
            do(p, "scalar", lambda e, s=s: e.activation(s[:, 5:6], s[:, 7:8], AF.Ln, bias=np.float32(EPS)))
            do(p, "scalar", lambda e, s=s: e.activation(s[:, 5:6], s[:, 5:6], AF.Exp, scale=-0.5),
               inc=sch)
            do(p, "scalar", lambda e, s=s: e.mul(s[:, 6:7], s[:, 2:3], s[:, 5:6]),
               waits=[(sch, sch.n)], inc=sch)
            aw = [(sems["p1tt"], ND * (tc - 1))] if tc >= 2 else []
            do(p, "scalar", lambda e, par=par, par4x=par4x, s=s: e.activation(xnb[par][:, :], xbuf[par4x][:, :], AF.Identity,
                                                                bias=s[:, 6:7], scale=s[:, 5:6]),
               inc=sems["p1ap"], waits=aw + [(sch, sch.n)])
            for ic in range(ND):
                gidx = tc * ND + ic
                w = [(sems["p1ap"], tc + 1)] if ic == 0 else []
                if gidx >= 2:
                    w.append((sems["p1tx"], gidx - 1))
                do(p, "tensor",
                   lambda e, ic=ic, par=par, gidx=gidx: e.transpose(
                       pst[:, (gidx % 2) * 1024:(gidx % 2) * 1024 + 128],
                       xnb[par][:, ic * 128:(ic + 1) * 128], ident[:, :]),
                   inc=sems["p1tt"], waits=w)
                do(p, "vector",
                   lambda e, tc=tc, ic=ic, gidx=gidx: e.tensor_copy(
                       xn1T[:, ic * T + tc * 128: ic * T + (tc + 1) * 128],
                       pst[:, (gidx % 2) * 1024:(gidx % 2) * 1024 + 128]),
                   inc=sems["p1tx"], waits=[(sems["p1tt"], gidx + 1)])
            if tc + 3 < NT:
                p1_load(tc + 3)
            if 6 <= tc and (tc - 6) < ND:
                slab0_tr(tc - 6)
        for ic in range(s0_emitted[0], ND):
            slab0_tr(ic)
        run_phase(p)
        st1p.close()
        p1st_total = sems["p1st"].n
        tq_done = [(sems["tq0"], sems["tq0"].n), (sems["tq1"], sems["tq1"].n)]
        if phases == 1:
            drain_all()
            st1.close()
            return nc

        HW = D // 2
        wS = [wS0] + [en1(nc.sbuf_tensor(f"wS{i}", [128, ND * HW], BF16)) for i in range(1, 3)]
        gob = [en1(nc.sbuf_tensor(f"gob{i}", [128, D], BF16)) for i in range(2)]
        psA = en1(nc.psum_tensor("psA", [128, max(T, D)], F32))
        psB = en1(nc.psum_tensor("psB", [128, max(T, D)], F32))
        psP2 = [psA, psB]

        p = new_prog()
        for wname in ("wk", "wv"):
            for i in range(NC4):
                sl = slice(i * CR, (i + 1) * CR)
                do(p, "gpsimd",
                   lambda e, wn=wname, sl=sl: e.dma_start(s_w[wn][sl, :], w_in[wn][sl, :]),
                   inc=sems["gcast"], inc_k=16)
            gc_after[wname] = sems["gcast"].n
        emit_ternary_fullrow()
        mmcnt, evcnt, stcnt = sems["p2mm"], sems["p2ev"], sems["p2st"]
        store_done = []
        unit_done = []

        p1tx_total = sems["p1tx"].n
        UNITS = [("wq", 0, "qk"), ("wk", 0, "qk"), ("wv", 0, "v"),
                 ("wq", 1, "qk"), ("wk", 1, "qk"), ("wv", 1, "v")]
        tr_need = []

        def emit_slab(u):
            wname, oh, _kind = UNITS[u]
            slab = u % 3
            for ic in range(ND):
                w = []
                if ic == 0:
                    w.append((sems["gcast"], gc_after[wname]))
                    if u >= 3:
                        w.append((mmcnt, unit_done[u - 3]))
                do(p, "sync",
                   lambda e, wn=wname, oh=oh, ic=ic, slab=slab: e.dma_start(
                       wS[slab][:, ic * HW:(ic + 1) * HW],
                       s_w[wn][oh * HW:(oh + 1) * HW, ic * 128:(ic + 1) * 128],
                       transpose=True),
                   inc=sems["p2tr"], inc_k=16, waits=w)
            tr_need.append(sems["p2tr"].n)

        def unit_gemm(u):
            wname, oh, kind = UNITS[u]
            slab = u % 3
            dst = {"wq": s_qt, "wk": s_kt, "wv": s_v}[wname]
            if kind == "qk":
                nq4 = max(1, T // PQ)
                pc4 = min(PQ, T)
                for oc8 in range(HW // 128):
                    oc = oh * (HW // 128) + oc8
                    E = evcnt.n
                    base_mm = mmcnt.n
                    for ic in range(ND):
                        for tq4 in range(nq4):
                            w = []
                            if oc8 == 0 and tq4 == 0 and ic == 0:
                                w.append((sems["p2tr"], tr_need[u]))
                                w.append((sems["p1tx"], p1tx_total))
                            if tq4 == 0 and ic == 0 and E >= 2:
                                w.append((evcnt, E - 1))
                            last = (tq4 == nq4 - 1 and ic == ND - 1)
                            do(p, "tensor",
                               lambda e, oc8=oc8, tq4=tq4, ic=ic, slab=slab, E=E, pc4=pc4: e.matmul(
                                   psP2[E % 2][:, tq4 * pc4:(tq4 + 1) * pc4],
                                   wS[slab][:, ic * HW + oc8 * 128: ic * HW + (oc8 + 1) * 128],
                                   xn1T[:, ic * T + tq4 * pc4: ic * T + (tq4 + 1) * pc4],
                                   start=(ic == 0), stop=(ic == ND - 1)),
                               inc=(mmcnt if last else None), waits=w)
                    ew = [(mmcnt, mmcnt.n)]
                    if E >= 2:
                        ew.append((stcnt, store_done[E - 2]))
                    do(p, "scalar",
                       lambda e, E=E: e.copy(gob[E % 2][:, 0:T], psP2[E % 2][:, 0:T]),
                       inc=evcnt, waits=ew)
                    do(p, "sync",
                       lambda e, oc=oc, E=E: e.dma_start(
                           dst[oc * 128:(oc + 1) * 128, :], gob[E % 2][:, 0:T]),
                       inc=stcnt, inc_k=16, waits=[(evcnt, evcnt.n)])
                    store_done.append(stcnt.n)
            else:
                nv2 = max(1, HW // PQ)
                pcv = min(PQ, HW)
                for tc in range(NT):
                    E = evcnt.n
                    for ic in range(ND):
                        for ov2 in range(nv2):
                            w = []
                            if tc == 0 and ic == 0 and ov2 == 0:
                                w.append((sems["p2tr"], tr_need[u]))
                                w.append((sems["p1tx"], p1tx_total))
                            if ic == 0 and ov2 == 0 and E >= 2:
                                w.append((evcnt, E - 1))
                            last = (ic == ND - 1 and ov2 == nv2 - 1)
                            do(p, "tensor",
                               lambda e, tc=tc, ic=ic, ov2=ov2, slab=slab, E=E, pcv=pcv: e.matmul(
                                   psP2[E % 2][:, ov2 * pcv:(ov2 + 1) * pcv],
                                   xn1T[:, ic * T + tc * 128: ic * T + (tc + 1) * 128],
                                   wS[slab][:, ic * HW + ov2 * pcv: ic * HW + (ov2 + 1) * pcv],
                                   start=(ic == 0), stop=(ic == ND - 1)),
                               inc=(mmcnt if last else None), waits=w)
                    ew = [(mmcnt, mmcnt.n)]
                    if E >= 2:
                        ew.append((stcnt, store_done[E - 2]))
                    do(p, "scalar",
                       lambda e, E=E: e.copy(gob[E % 2][:, 0:HW], psP2[E % 2][:, 0:HW]),
                       inc=evcnt, waits=ew)
                    do(p, "sync",
                       lambda e, tc=tc, oh=oh, E=E: e.dma_start(
                           s_v[tc * 128:(tc + 1) * 128, oh * HW:(oh + 1) * HW],
                           gob[E % 2][:, 0:HW]),
                       inc=stcnt, inc_k=16, waits=[(evcnt, evcnt.n)])
                    store_done.append(stcnt.n)
            unit_done.append(mmcnt.n)

        tr_need.append(sems["p2tr"].n)
        emit_slab(1)
        emit_slab(2)
        for u in range(len(UNITS)):
            unit_gemm(u)
            if u + 3 < len(UNITS):
                emit_slab(u + 3)

        for i in range(NC4):
            sl = slice(i * CR, (i + 1) * CR)
            do(p, "gpsimd", lambda e, sl=sl: e.dma_start(s_w["wo"][sl, :], w_in["wo"][sl, :]),
               inc=sems["gcast"], inc_k=16)
        gc_wo = sems["gcast"].n

        run_phase(p)
        st1.close()
        p2st_total = stcnt.n
        if phases == 2:
            drain_all()
            return nc

        st2 = ExitStack()
        zT = st2.enter_context(nc.sbuf_tensor("zT", [128, NH * T], BF16))
        stg = st2.enter_context(nc.sbuf_tensor("stg", [128, D], BF16))
        st3 = ExitStack()
        en3 = st3.enter_context
        qt_h = [en3(nc.sbuf_tensor(f"qth{i}", [128, T], BF16)) for i in range(2)]
        kt_h = [en3(nc.sbuf_tensor(f"kth{i}", [128, T], BF16)) for i in range(2)]
        v_h = [en3(nc.sbuf_tensor(f"vh{i}", [128, NT * 128], BF16)) for i in range(4)]
        put = [en3(nc.sbuf_tensor(f"put{i}", [128, PTOT], BF16)) for i in range(2)]
        rcl = [en3(nc.sbuf_tensor(f"rcl{i}", [128, T], F32)) for i in range(2)]
        ps_s = en3(nc.psum_tensor("ps_s", [128, 3072], F32))
        ps_lz = en3(nc.psum_tensor("ps_lz", [128, 1024], F32))


        p = new_prog()
        hd_sem = [sems["p3hd0"], sems["p3hd1"]]
        hv_sem = [sems["p3hv0"], sems["p3hv1"], sems["p3hv2"], sems["p3hv3"]]
        schunks = 0
        slot_req = {}
        lz_free = {}
        lz_cursor = [0]
        NQG = (T + 511) // 512
        prev_lz = []
        prev_v = []

        def riffle(a, b):
            out_ = []
            ia = ib = 0
            while ia < len(a) or ib < len(b):
                while ib < len(b) and (ia >= len(a) or ib * max(1, len(a)) < ia * len(b)):
                    out_.append(b[ib]); ib += 1
                if ia < len(a):
                    out_.append(a[ia]); ia += 1
            return out_

        for h in range(NH):
            par = h % 2
            s_list = []
            lz_list = []
            v_list = []
            par4 = h % 4
            lw = [(stcnt, p2st_total)] if h == 0 else []
            if h >= 2:
                lw.append((sems["p3zev"], NQG * (h - 1)))
            do(p, "sync", lambda e, h=h, par=par: e.dma_start(qt_h[par][:, :], s_qt[h * 128:(h + 1) * 128, :]),
               inc=hd_sem[par], inc_k=16, waits=lw)
            do(p, "sync", lambda e, h=h, par=par: e.dma_start(kt_h[par][:, :], s_kt[h * 128:(h + 1) * 128, :]),
               inc=hd_sem[par], inc_k=16)
            vw4 = [(sems["p3zev"], NQG * (h - 3))] if h >= 4 else []
            for kc in range(NT):
                do(p, "sync", lambda e, h=h, par4=par4, kc=kc: e.dma_start(
                    v_h[par4][:, kc * 128:(kc + 1) * 128],
                    s_v[kc * 128:(kc + 1) * 128, h * 128:(h + 1) * 128]),
                   inc=hv_sem[par4], inc_k=16, waits=vw4 if kc == 0 else ())
            ld_need = 32 * (h // 2 + 1)
            vld_need = NT * 16 * (h // 4 + 1)
            if h < ND:
                do(p, "sync",
                   lambda e, ic=h: e.dma_start(stg[:, :], s_w["wo"][:, ic * 128:(ic + 1) * 128],
                                               transpose=True),
                   inc=sems["p3ws"], inc_k=16,
                   waits=[(sems["gcast"], gc_wo), (sems["p3wt"], 16 * h)])
                do(p, "sync",
                   lambda e, ic=h: e.dma_start(s_woT[ic * 128:(ic + 1) * 128, :], stg[:, :]),
                   inc=sems["p3wt"], inc_k=16, waits=[(sems["p3ws"], 16 * (h + 1))])

            for kc in range(NT):
                chs = pieces(kc * 128, T, kc * 128)
                if schunks % 2:
                    schunks += 1
                j = 0
                while j < len(chs):
                    pair = chs[j:j + 2]
                    s0 = schunks % 6
                    for (a, b) in pair:
                        slot = schunks % 6
                        w = []
                        if kc == 0 and a == chs[0][0]:
                            w.append((hd_sem[par], ld_need))
                        req = slot_req.get(slot, 0)
                        if req > 0:
                            w.append((sems["p3e"], req))
                        do(p, "tensor",
                           lambda e, kc=kc, a=a, b=b, slot=slot, par=par: e.matmul(
                               ps_s[:, slot * 512: slot * 512 + (b - a)],
                               kt_h[par][:, kc * 128:(kc + 1) * 128],
                               qt_h[par][:, a:b], start=True, stop=True),
                           inc=sems["p3s"], waits=w, sink=s_list)
                        schunks += 1
                    a0, b1 = pair[0][0], pair[-1][1]
                    ew = [(sems["p3s"], sems["p3s"].n)]
                    if h >= 2 and kc == 0 and j == 0:
                        ew.append((sems["p3zev"], NQG * (h - 1)))
                    do(p, "scalar",
                       lambda e, kc=kc, a0=a0, b1=b1, s0=s0, par=par: e.activation(
                           put[par][:, poff[kc] + (a0 - kc * 128): poff[kc] + (b1 - kc * 128)],
                           ps_s[:, s0 * 512: s0 * 512 + (b1 - a0)],
                           AF.Exp, scale=SC, bias=np.float32(-CSHIFT)),
                       inc=sems["p3e"], waits=ew)
                    for (a, b) in pair:
                        slot_req[s0] = sems["p3e"].n
                        s0 = (s0 + 1) % 6
                    j += 2
                do(p, "gpsimd",
                   lambda e, kc=kc, par=par: e.affine_select(
                       put[par][:, poff[kc]:poff[kc] + 128],
                       put[par][:, poff[kc]:poff[kc] + 128],
                       [[1, 128]], ALU.is_ge, 0.0, base=0, channel_multiplier=-1),
                   inc=sems["p3m"], waits=[(sems["p3e"], sems["p3e"].n)])

            NQ = (T + 511) // 512
            for q in range(NQ):
                qlo, qhi = q * 512, min((q + 1) * 512, T)
                gq = h * NQ + q
                slot = lz_cursor[0] % 2 * 512
                kcs_q = [kc for kc in range(NT) if kc * 128 < qhi]
                segs = []
                for kc in kcs_q:
                    lo = max(qlo, kc * 128)
                    cut = min(max((kc + 1) * 128, lo), qhi)
                    for (lo2, hi2, stp) in ((lo, cut, True), (cut, qhi, False)):
                        for (a, b) in pieces(lo2, hi2, qlo):
                            segs.append((kc, a, b, stp))
                kmax_q = min((qhi + 127) // 128, NT)
                for idx, (kc, a, b, stp) in enumerate(segs):
                    w = []
                    if idx == 0:
                        w.append((sems["p3m"], NT * h + kmax_q))
                        fr = lz_free.get(lz_cursor[0] % 2)
                        if fr is not None:
                            w.append(fr)
                    do(p, "tensor",
                       lambda e, kc=kc, a=a, b=b, par=par, qlo=qlo, slot=slot, stp=stp: e.matmul(
                           ps_lz[:, slot + (a - qlo): slot + (b - qlo)], ones_sb[:, :],
                           put[par][:, poff[kc] + (a - kc * 128): poff[kc] + (b - kc * 128)],
                           start=(kc == 0), stop=stp, skip_group_check=True),
                       inc=(sems["p3l"] if idx == len(segs) - 1 else None), waits=w, sink=lz_list)
                do(p, "vector",
                   lambda e, par=par, qlo=qlo, qhi=qhi, slot=slot: e.reciprocal(
                       rcl[par][:, qlo:qhi], ps_lz[:, slot: slot + (qhi - qlo)]),
                   inc=sems["p3rc"], waits=[(sems["p3l"], gq + 1)], sink=v_list)
                lz_free[lz_cursor[0] % 2] = (sems["p3rc"], sems["p3rc"].n)
                lz_cursor[0] += 1

            for q in range(NQ):
                qlo, qhi = q * 512, min((q + 1) * 512, T)
                gq = h * NQ + q
                slot = lz_cursor[0] % 2 * 512
                kcs_q = [kc for kc in range(NT) if kc * 128 < qhi]
                segs = []
                for kc in kcs_q:
                    lo = max(qlo, kc * 128)
                    cut = min(max((kc + 1) * 128, lo), qhi)
                    for (lo2, hi2, stp) in ((lo, cut, True), (cut, qhi, False)):
                        for (a, b) in pieces(lo2, hi2, qlo):
                            segs.append((kc, a, b, stp))
                kmax_q = min((qhi + 127) // 128, NT)
                for idx, (kc, a, b, stp) in enumerate(segs):
                    w = []
                    if idx == 0:
                        w.append((sems["p3m"], NT * h + kmax_q))
                        if q == 0:
                            w.append((hv_sem[par4], vld_need))
                        fr = lz_free.get(lz_cursor[0] % 2)
                        if fr is not None:
                            w.append(fr)
                    do(p, "tensor",
                       lambda e, kc=kc, a=a, b=b, par=par, par4=par4, qlo=qlo, slot=slot, stp=stp: e.matmul(
                           ps_lz[:, slot + (a - qlo): slot + (b - qlo)],
                           v_h[par4][:, kc * 128:(kc + 1) * 128],
                           put[par][:, poff[kc] + (a - kc * 128): poff[kc] + (b - kc * 128)],
                           start=(kc == 0), stop=stp, skip_group_check=True),
                       inc=(sems["p3z"] if idx == len(segs) - 1 else None), waits=w, sink=lz_list)
                do(p, "vector",
                   lambda e, h=h, par=par, qlo=qlo, qhi=qhi, slot=slot: e.tensor_tensor(
                       zT[:, h * T + qlo: h * T + qhi],
                       ps_lz[:, slot: slot + (qhi - qlo)], rcl[par][:, qlo:qhi], ALU.mult),
                   inc=sems["p3zev"], waits=[(sems["p3z"], gq + 1)], sink=v_list)
                lz_free[lz_cursor[0] % 2] = (sems["p3zev"], sems["p3zev"].n)
                lz_cursor[0] += 1

            p["tensor"] += riffle(s_list, prev_lz)
            prev_lz = lz_list
            p["vector"] += prev_v
            prev_v = v_list
        p["tensor"] += prev_lz
        p["vector"] += prev_v

        run_phase(p)
        st3.close()
        tq_tot = [sems["tq0"].n, sems["tq1"].n]
        if phases == 3:
            drain_all()
            st2.close()
            return nc

        st4 = ExitStack()
        en4 = st4.enter_context
        woT = en4(nc.sbuf_tensor("woT", [128, ND * D], BF16))
        xrb = [en4(nc.sbuf_tensor(f"xrb{i}", [128, D], F32)) for i in range(2)]
        scrap4 = en4(nc.sbuf_tensor("scrap4", [128, D], F32))
        x2b = en4(nc.sbuf_tensor("x2b", [128, D], F32))
        xn2b = [en4(nc.sbuf_tensor(f"xn2b{i}", [128, D], BF16)) for i in range(2)]
        stt4 = [en4(nc.sbuf_tensor(f"p4s{i}", [128, 8], F32)) for i in range(2)]
        ps4 = [en4(nc.psum_tensor(f"ps4_{i}", [128, D], F32)) for i in range(2)]

        p = new_prog()
        p3wt_total = sems["p3wt"].n
        for ic in range(ND):
            do(p, "sync",
               lambda e, ic=ic: e.dma_start(woT[:, ic * D:(ic + 1) * D],
                                            s_woT[ic * 128:(ic + 1) * 128, :]),
               inc=sems["p4tr"], inc_k=16,
               waits=[(sems["p3wt"], p3wt_total)] if ic == 0 else ())
        for ic in range(ND):
            do(p, "sync",
               lambda e, ic=ic: e.dma_start(stg[:, :], s_qf[:, ic * 128:(ic + 1) * 128],
                                            transpose=True),
               inc=sems["p4qs"], inc_k=16,
               waits=([(sems["tq0"], tq_tot[0]), (sems["tq1"], tq_tot[1])] if ic == 0 else [])
                     + [(sems["p4qt"], 16 * ic)])
            do(p, "sync",
               lambda e, ic=ic: e.dma_start(s_qfT[ic * 128:(ic + 1) * 128, :], stg[:, :]),
               inc=sems["p4qt"], inc_k=16, waits=[(sems["p4qs"], 16 * (ic + 1))])
        for ic in range(ND):
            do(p, "sync",
               lambda e, ic=ic: e.dma_start(stg[:, 0:TH],
                                            s_xn2[0:TH, ic * 128:(ic + 1) * 128], transpose=True),
               inc=sems["p4xs"], inc_k=16,
               waits=([(sems["p4sn"], 16 * (TH // 128)), (sems["p4qt"], 16 * ND)]
                      if ic == 0 else [])
                     + [(sems["p4xt"], 16 * ic)])
            do(p, "sync",
               lambda e, ic=ic: e.dma_start(s_xn2T[ic * 128:(ic + 1) * 128, 0:TH], stg[:, 0:TH]),
               inc=sems["p4xt"], inc_k=16, waits=[(sems["p4xs"], 16 * (ic + 1))])
        p4xl = [sems["p4xl0"], sems["p4xl1"]]

        def p4_load(tc):
            par = tc % 2
            w = [(sems["p4x2"], tc - 1)] if tc >= 2 else []
            do(p, "gpsimd",
               lambda e, tc=tc, par=par: e.dma_start(xrb[par][:, :], x_in[tc * 128:(tc + 1) * 128, :]),
               inc=p4xl[par], inc_k=16, waits=w)

        p4_load(0)
        if NT > 1:
            p4_load(1)
        nq = max(1, D // PQ)
        pc = min(PQ, D)
        red4_base = sems["red"].n
        if p4lite in (10, 11):
            for tc in range(NT):
                par = tc % 2
                s = stt4[par]
                w = [(sems["p4tr"], 16 * ND)]
                if tc >= 2:
                    w.append((sems["p4ap"], tc - 1))
                for ic in range(ND):
                    for oq in range(nq):
                        last = (ic == ND - 1 and oq == nq - 1)
                        do(p, "tensor",
                           lambda e, tc=tc, par=par, ic=ic, oq=oq: e.matmul(
                               ps4[par][:, oq * pc:(oq + 1) * pc],
                               zT[:, ic * T + tc * 128: ic * T + (tc + 1) * 128],
                               woT[:, ic * D + oq * pc: ic * D + (oq + 1) * pc],
                               start=(ic == 0), stop=(ic == ND - 1)),
                           inc=(sems["p4mm"] if last else None),
                           waits=(w if (ic == 0 and oq == 0) else ()))
                vch = sems["vch"]
                sch = sems["sch"]
                if p4lite == 11:
                    vw = [(sems["p4mm"], tc + 1)]
                    if tc >= 2:
                        vw.append((sems["p4ap"], tc - 1))
                    do(p, "vector", lambda e, par=par, s=s: e.reduce_sum(s[:, 0:1], ps4[par][:, :], axis=AXX),
                       waits=vw, inc=vch)
                    do(p, "vector", lambda e, s=s: e.tensor_scalar(s[:, 2:3], s[:, 0:1], -1.0 / D, None, ALU.mult),
                       waits=[(vch, vch.n)], inc=vch)
                    do(p, "vector", lambda e, s=s: e.tensor_tensor(s[:, 3:4], s[:, 2:3], s[:, 2:3], ALU.mult),
                       waits=[(vch, vch.n)], inc=vch)
                do(p, "scalar", lambda e, par=par, s=s: e.activation(scrap4[:, :], ps4[par][:, :], AF.Square,
                                                                    accum_out=s[:, 1:2]),
                   inc=sems["p4sq"], waits=[(sems["p4mm"], tc + 1)])
                if p4lite == 11:
                    do(p, "vector", lambda e, s=s: e.tensor_scalar(s[:, 4:5], s[:, 1:2], 1.0 / D, s[:, 3:4],
                                                                   ALU.mult, ALU.subtract),
                       inc=sems["p4var"], waits=[(sems["p4sq"], tc + 1), (vch, vch.n)])
                    do(p, "scalar", lambda e, s=s: e.activation(s[:, 5:6], s[:, 4:5], AF.Ln, bias=np.float32(EPS)),
                       waits=[(sems["p4var"], tc + 1)])
                else:
                    do(p, "scalar", lambda e, s=s: e.activation(s[:, 5:6], s[:, 1:2], AF.Ln, bias=np.float32(1.0)),
                       waits=[(sems["p4sq"], tc + 1)])
                do(p, "scalar", lambda e, s=s: e.activation(s[:, 5:6], s[:, 5:6], AF.Exp, scale=-0.5),
                   inc=sch)
                do(p, "scalar", lambda e, s=s: e.mul(s[:, 6:7], s[:, 2:3] if p4lite == 11 else s[:, 5:6], s[:, 5:6]),
                   waits=[(sch, sch.n)], inc=sch)
                do(p, "scalar", lambda e, par=par, s=s: e.activation(scrap4[:, :], ps4[par][:, :], AF.Identity,
                                                                    bias=s[:, 6:7], scale=s[:, 5:6]),
                   inc=sems["p4ap"], waits=[(sch, sch.n)])
            run_phase(p)
            drain_all()
            st4.close()
            st2.close()
            return nc
        if p4lite in (8, 9):
            for tc in range(NT):
                par = tc % 2
                s = stt4[par]
                w = [(sems["p4tr"], 16 * ND)]
                if tc >= 2:
                    w.append((sems["p4ap"], tc - 1))
                for ic in range(ND):
                    for oq in range(nq):
                        last = (ic == ND - 1 and oq == nq - 1)
                        do(p, "tensor",
                           lambda e, tc=tc, par=par, ic=ic, oq=oq: e.matmul(
                               ps4[par][:, oq * pc:(oq + 1) * pc],
                               zT[:, ic * T + tc * 128: ic * T + (tc + 1) * 128],
                               woT[:, ic * D + oq * pc: ic * D + (oq + 1) * pc],
                               start=(ic == 0), stop=(ic == ND - 1)),
                           inc=(sems["p4mm"] if last else None),
                           waits=(w if (ic == 0 and oq == 0) else ()))
                if p4lite == 8:
                    do(p, "vector", lambda e, par=par, s=s: e.reduce_sum(s[:, 0:1], ps4[par][:, :], axis=AXX),
                       waits=[(sems["p4mm"], tc + 1)], inc=sems["p4ap"])
                else:
                    do(p, "scalar", lambda e, par=par, s=s: e.activation(scrap4[:, :], ps4[par][:, :], AF.Square,
                                                                        accum_out=s[:, 1:2]),
                       inc=sems["p4ap"], waits=[(sems["p4mm"], tc + 1)])
            run_phase(p)
            drain_all()
            st4.close()
            st2.close()
            return nc
        if p4lite == 6:
            for tc in range(NT):
                par = tc % 2
                s = stt4[par]
                w = [(sems["p4tr"], 16 * ND)]
                if tc >= 2:
                    w.append((sems["p4ap"], tc - 1))
                for ic in range(ND):
                    for oq in range(nq):
                        last = (ic == ND - 1 and oq == nq - 1)
                        do(p, "tensor",
                           lambda e, tc=tc, par=par, ic=ic, oq=oq: e.matmul(
                               ps4[par][:, oq * pc:(oq + 1) * pc],
                               zT[:, ic * T + tc * 128: ic * T + (tc + 1) * 128],
                               woT[:, ic * D + oq * pc: ic * D + (oq + 1) * pc],
                               start=(ic == 0), stop=(ic == ND - 1)),
                           inc=(sems["p4mm"] if last else None),
                           waits=(w if (ic == 0 and oq == 0) else ()))
                vch = sems["vch"]
                vw = [(sems["p4mm"], tc + 1)]
                if tc >= 2:
                    vw.append((sems["p4ap"], tc - 1))
                do(p, "vector", lambda e, par=par, s=s: e.reduce_sum(s[:, 0:1], ps4[par][:, :], axis=AXX),
                   waits=vw, inc=vch)
                do(p, "vector", lambda e, s=s: e.tensor_scalar(s[:, 2:3], s[:, 0:1], -1.0 / D, None, ALU.mult),
                   waits=[(vch, vch.n)], inc=vch)
                do(p, "vector", lambda e, s=s: e.tensor_tensor(s[:, 3:4], s[:, 2:3], s[:, 2:3], ALU.mult),
                   waits=[(vch, vch.n)], inc=vch)
                do(p, "scalar", lambda e, par=par, s=s: e.activation(scrap4[:, :], ps4[par][:, :], AF.Square,
                                                                    accum_out=s[:, 1:2]),
                   inc=sems["p4sq"], waits=[(sems["p4mm"], tc + 1)])
                do(p, "vector", lambda e, s=s: e.tensor_scalar(s[:, 4:5], s[:, 1:2], 1.0 / D, s[:, 3:4],
                                                               ALU.mult, ALU.subtract),
                   inc=sems["p4var"], waits=[(sems["p4sq"], tc + 1), (vch, vch.n)])
                do(p, "scalar", lambda e, s=s: e.activation(s[:, 5:6], s[:, 4:5], AF.Ln, bias=np.float32(EPS)),
                   waits=[(sems["p4var"], tc + 1)])
                sch = sems["sch"]
                do(p, "scalar", lambda e, s=s: e.activation(s[:, 5:6], s[:, 5:6], AF.Exp, scale=-0.5),
                   inc=sch)
                do(p, "scalar", lambda e, s=s: e.mul(s[:, 6:7], s[:, 2:3], s[:, 5:6]),
                   waits=[(sch, sch.n)], inc=sch)
                aw = [(sch, sch.n)]
                if tc >= 1:
                    aw += [(sems["p4var"], tc), (sems["p4s2"], 16 * tc)]
                do(p, "scalar", lambda e, par=par, s=s: e.activation(x2b[:, :], ps4[par][:, :], AF.Identity,
                                                                    bias=s[:, 6:7], scale=s[:, 5:6]),
                   inc=sems["p4ap"], waits=aw)
                do(p, "vector", lambda e, par=par: e.tensor_tensor(x2b[:, :], x2b[:, :], xrb[par][:, :], ALU.add),
                   inc=sems["p4x2"],
                   waits=[(sems["p4ap"], tc + 1), (p4xl[par], 16 * (tc // 2 + 1))])
                do(p, "gpsimd", lambda e, tc=tc: e.dma_start(s_x2[tc * 128:(tc + 1) * 128, :], x2b[:, :]),
                   inc=sems["p4s2"], inc_k=16, waits=[(sems["p4x2"], tc + 1)])
                if tc + 2 < NT:
                    p4_load(tc + 2)
            run_phase(p)
            drain_all()
            st4.close()
            st2.close()
            return nc
        if p4lite in (3, 4, 5):
            run_phase(p)
            drain_all()
            st4.close()
            st2.close()
            return nc
        if p4lite:
            for tc in range(NT):
                par = tc % 2
                w = [(sems["p4tr"], 16 * ND)]
                if tc >= 2:
                    w.append((sems["p4ap"], tc - 1))
                if p4lite == 2:
                    for ic in range(ND):
                        for oq in range(nq):
                            last = (ic == ND - 1 and oq == nq - 1)
                            do(p, "tensor",
                               lambda e, tc=tc, par=par, ic=ic, oq=oq: e.matmul(
                                   ps4[par][:, oq * pc:(oq + 1) * pc],
                                   woT[:, ic * D + tc * 128: ic * D + tc * 128 + 128],
                                   woT[:, ic * D + oq * pc: ic * D + (oq + 1) * pc],
                                   start=(ic == 0), stop=(ic == ND - 1)),
                               inc=(sems["p4mm"] if last else None),
                               waits=(w if (ic == 0 and oq == 0) else ()))
                else:
                    for ic in range(ND):
                        for oq in range(nq):
                            last = (ic == ND - 1 and oq == nq - 1)
                            do(p, "tensor",
                               lambda e, tc=tc, par=par, ic=ic, oq=oq: e.matmul(
                                   ps4[par][:, oq * pc:(oq + 1) * pc],
                                   zT[:, ic * T + tc * 128: ic * T + (tc + 1) * 128],
                                   woT[:, ic * D + oq * pc: ic * D + (oq + 1) * pc],
                                   start=(ic == 0), stop=(ic == ND - 1)),
                               inc=(sems["p4mm"] if last else None),
                               waits=(w if (ic == 0 and oq == 0) else ()))
                do(p, "scalar", lambda e, par=par: e.copy(scrap4[:, :], ps4[par][:, :]),
                   inc=sems["p4ap"], waits=[(sems["p4mm"], tc + 1)])
            run_phase(p)
            drain_all()
            st4.close()
            st2.close()
            return nc
        for tc in range(NT):
            par = tc % 2
            s = stt4[par]
            w = [(sems["p4tr"], 16 * ND)]
            if tc >= 2:
                w.append((sems["p4ap"], tc - 1))
            for ic in range(ND):
                for oq in range(nq):
                    last = (ic == ND - 1 and oq == nq - 1)
                    do(p, "tensor",
                       lambda e, tc=tc, par=par, ic=ic, oq=oq: e.matmul(
                           ps4[par][:, oq * pc:(oq + 1) * pc],
                           zT[:, ic * T + tc * 128: ic * T + (tc + 1) * 128],
                           woT[:, ic * D + oq * pc: ic * D + (oq + 1) * pc],
                           start=(ic == 0), stop=(ic == ND - 1)),
                       inc=(sems["p4mm"] if last else None),
                       waits=(w if (ic == 0 and oq == 0) else ()))
            vw = [(sems["p4mm"], tc + 1)]
            if tc >= 2:
                vw.append((sems["p4xn"], tc - 1))
            red, sch = sems["red"], sems["sch"]
            do(p, "vector", lambda e, par=par, s=s: e.reduce_sum(s[:, 0:1], ps4[par][:, :], axis=AXX),
               waits=vw, inc=red)
            do(p, "scalar", lambda e, par=par, s=s: e.activation(scrap4[:, :], ps4[par][:, :], AF.Square,
                                                                accum_out=s[:, 1:2]),
               inc=sch, waits=[(sems["p4mm"], tc + 1), (red, red4_base + tc + 1),
                               (sems["p4xn"], tc - 1)])
            sqv = sch.n
            do(p, "scalar", lambda e, s=s: e.mul(s[:, 2:3], s[:, 0:1], -1.0 / D))
            do(p, "scalar", lambda e, s=s: e.activation(s[:, 3:4], s[:, 2:3], AF.Square))
            do(p, "scalar", lambda e, s=s: e.mul(s[:, 4:5], s[:, 3:4], -1.0), inc=sch)
            negv = sch.n
            do(p, "scalar", lambda e, s=s: e.mul(s[:, 7:8], s[:, 1:2], 1.0 / D),
               waits=[(sch, sqv)])
            do(p, "scalar", lambda e, s=s: e.add(s[:, 7:8], s[:, 7:8], s[:, 4:5]),
               waits=[(sch, negv)])
            do(p, "scalar", lambda e, s=s: e.activation(s[:, 5:6], s[:, 7:8], AF.Ln, bias=np.float32(EPS)))
            do(p, "scalar", lambda e, s=s: e.activation(s[:, 5:6], s[:, 5:6], AF.Exp, scale=-0.5),
               inc=sch)
            do(p, "scalar", lambda e, s=s: e.mul(s[:, 6:7], s[:, 2:3], s[:, 5:6]),
               waits=[(sch, sch.n)], inc=sch)
            aw = [(sch, sch.n)]
            if tc >= 1:
                aw += [(sems["p4xn"], tc), (sems["p4s2"], 16 * tc)]
            do(p, "scalar", lambda e, par=par, s=s: e.activation(x2b[:, :], ps4[par][:, :], AF.Identity,
                                                                bias=s[:, 6:7], scale=s[:, 5:6]),
               inc=sems["p4ap"], waits=aw)
            do(p, "vector", lambda e, par=par: e.tensor_tensor(x2b[:, :], x2b[:, :], xrb[par][:, :], ALU.add),
               inc=sems["p4x2"],
               waits=[(sems["p4ap"], tc + 1), (p4xl[par], 16 * (tc // 2 + 1))])
            do(p, "gpsimd", lambda e, tc=tc: e.dma_start(s_x2[tc * 128:(tc + 1) * 128, :], x2b[:, :]),
               inc=sems["p4s2"], inc_k=16, waits=[(sems["p4x2"], tc + 1)])
            vch = sems["vch"]
            do(p, "vector", lambda e, s=s: e.reduce_sum(s[:, 0:1], x2b[:, :], axis=AXX),
               waits=[(sems["p4x2"], tc + 1)], inc=vch)
            do(p, "scalar", lambda e, s=s: e.activation(scrap4[:, :], x2b[:, :], AF.Square,
                                                        accum_out=s[:, 1:2]),
               inc=sems["p4sq"], waits=[(sems["p4x2"], tc + 1)])
            do(p, "vector", lambda e, s=s: e.tensor_scalar(s[:, 2:3], s[:, 0:1], -1.0 / D, None, ALU.mult),
               waits=[(vch, vch.n)], inc=vch)
            do(p, "vector", lambda e, s=s: e.tensor_tensor(s[:, 3:4], s[:, 2:3], s[:, 2:3], ALU.mult),
               waits=[(vch, vch.n)], inc=vch)
            do(p, "vector", lambda e, s=s: e.tensor_scalar(s[:, 4:5], s[:, 1:2], 1.0 / D, s[:, 3:4],
                                                           ALU.mult, ALU.subtract),
               waits=[(vch, vch.n), (sems["p4sq"], sems["p4sq"].n)], inc=sems["p4var"])
            do(p, "scalar", lambda e, s=s: e.activation(s[:, 5:6], s[:, 4:5], AF.Ln, bias=np.float32(EPS)),
               waits=[(sems["p4var"], sems["p4var"].n)])
            do(p, "scalar", lambda e, s=s: e.activation(s[:, 5:6], s[:, 5:6], AF.Exp, scale=-0.5),
               inc=sems["p4rs"])
            xw = [(sems["p4sn"], 16 * (tc - 1))] if tc >= 2 else []
            do(p, "vector", lambda e, par=par, s=s: e.tensor_scalar(xn2b[par][:, :], x2b[:, :],
                                                                    s[:, 2:3], s[:, 5:6],
                                                                    ALU.add, ALU.mult),
               inc=sems["p4xn"], waits=xw + [(sems["p4rs"], sems["p4rs"].n)])
            do(p, "gpsimd", lambda e, tc=tc, par=par: e.dma_start(s_xn2[tc * 128:(tc + 1) * 128, :],
                                                                  xn2b[par][:, :]),
               inc=sems["p4sn"], inc_k=16, waits=[(sems["p4xn"], tc + 1)])
            if tc + 2 < NT:
                p4_load(tc + 2)

        run_phase(p)
        st4.close()
        st2.close()
        p4sn_total = sems["p4sn"].n
        if phases == 4:
            drain_all()
            return nc
        p4s2_total = sems["p4s2"].n

        st5 = ExitStack()
        en5 = st5.enter_context
        qT = en5(nc.sbuf_tensor("qT", [128, ND * D], BF16))
        xn2T = en5(nc.sbuf_tensor("xn2T", [128, ND * T], BF16))
        biasB = en5(nc.sbuf_tensor("biasB", [128, D], F32))
        ybuf = [en5(nc.sbuf_tensor(f"yb{i}", [128, D], F32)) for i in range(2)]
        x2rb = [en5(nc.sbuf_tensor(f"x2rb{i}", [128, D], F32)) for i in range(2)]
        scrap5 = en5(nc.sbuf_tensor("scrap5", [128, D], F32))
        stt5 = [en5(nc.sbuf_tensor(f"p5s{i}", [128, 8], F32)) for i in range(2)]
        ps5 = [en5(nc.psum_tensor(f"ps5_{i}", [128, D], F32)) for i in range(2)]

        p = new_prog()
        p4qt_total = sems["p4qt"].n
        p4xt_total = sems["p4xt"].n
        for ic in range(ND):
            do(p, "sync",
               lambda e, ic=ic: e.dma_start(qT[:, ic * D:(ic + 1) * D],
                                            s_qfT[ic * 128:(ic + 1) * 128, :]),
               inc=sems["p5tr"], inc_k=16,
               waits=[(sems["p4qt"], p4qt_total)] if ic == 0 else ())
        for ic in range(ND):
            do(p, "sync",
               lambda e, ic=ic: e.dma_start(xn2T[:, ic * T + TH:(ic + 1) * T],
                                            s_xn2[TH:T, ic * 128:(ic + 1) * 128], transpose=True),
               inc=sems["p5trs"], inc_k=16,
               waits=[(sems["p4sn"], p4sn_total)] if ic == 0 else ())
        for ic in range(ND):
            do(p, "gpsimd",
               lambda e, ic=ic: e.dma_start(xn2T[:, ic * T: ic * T + TH],
                                            s_xn2T[ic * 128:(ic + 1) * 128, 0:TH]),
               inc=sems["p5xs"], inc_k=16,
               waits=[(sems["p4xt"], p4xt_total)] if ic == 0 else ())
        do(p, "gpsimd", lambda e: e.dma_start(biasB[:, :], b_in.broadcast_to([128, D])),
           inc=sems["biasld"], inc_k=16)
        p5xl = [sems["p5xl0"], sems["p5xl1"]]
        for tc in range(NT):
            par = tc % 2
            w = [(sems["p4s2"], p4s2_total)] if tc == 0 else []
            if tc >= 2:
                w.append((sems["p5out"], 16 * (tc - 1)))
            do(p, "gpsimd",
               lambda e, tc=tc, par=par: e.dma_start(x2rb[par][:, :], s_x2[tc * 128:(tc + 1) * 128, :]),
               inc=p5xl[par], inc_k=16, waits=w)
        for tc in range(NT):
            par = tc % 2
            s = stt5[par]
            w = [(sems["p5tr"], 16 * ND), (sems["p5trs"], 16 * ND), (sems["p5xs"], 16 * ND)]
            if tc >= 2:
                w.append((sems["p5yb"], 2 * tc - 3))
            for ic in range(ND):
                for oq in range(nq):
                    last = (ic == ND - 1 and oq == nq - 1)
                    do(p, "tensor",
                       lambda e, tc=tc, par=par, ic=ic, oq=oq: e.matmul(
                           ps5[par][:, oq * pc:(oq + 1) * pc],
                           xn2T[:, ic * T + tc * 128: ic * T + (tc + 1) * 128],
                           qT[:, ic * D + oq * pc: ic * D + (oq + 1) * pc],
                           start=(ic == 0), stop=(ic == ND - 1)),
                       inc=(sems["p5mm"] if last else None),
                       waits=(w if (ic == 0 and oq == 0) else ()))
            yw = [(sems["p5mm"], tc + 1), (sems["biasld"], 16)]
            if tc >= 2:
                yw.append((sems["p5out"], 16 * (tc - 1)))
            vch = sems["vch"]
            do(p, "vector",
               lambda e, par=par, s=s: e.scalar_tensor_tensor(
                   ybuf[par][:, :], ps5[par][:, :], 1.0, biasB[:, :],
                   ALU.mult, ALU.add),
               inc=sems["p5yb"], waits=yw)
            do(p, "vector", lambda e, par=par, s=s: e.reduce_sum(s[:, 0:1], ybuf[par][:, :], axis=AXX),
               inc=vch, waits=[(sems["p5rs"], tc - 1)])
            do(p, "scalar", lambda e, par=par, s=s: e.activation(scrap5[:, :], ybuf[par][:, :], AF.Square,
                                                                accum_out=s[:, 1:2]),
               inc=sems["p5sq"], waits=[(sems["p5yb"], 2 * tc + 1), (sems["p5rs"], tc - 1)])
            do(p, "vector", lambda e, s=s: e.tensor_scalar(s[:, 2:3], s[:, 0:1], -1.0 / D, None, ALU.mult),
               waits=[(vch, vch.n)], inc=vch)
            do(p, "vector", lambda e, s=s: e.tensor_tensor(s[:, 3:4], s[:, 2:3], s[:, 2:3], ALU.mult),
               waits=[(vch, vch.n)], inc=vch)
            do(p, "vector", lambda e, s=s: e.tensor_scalar(s[:, 4:5], s[:, 1:2], 1.0 / D, s[:, 3:4],
                                                           ALU.mult, ALU.subtract),
               waits=[(vch, vch.n), (sems["p5sq"], sems["p5sq"].n)], inc=sems["p5var"])
            do(p, "scalar", lambda e, s=s: e.activation(s[:, 5:6], s[:, 4:5], AF.Ln, bias=np.float32(EPS)),
               waits=[(sems["p5var"], sems["p5var"].n)])
            do(p, "scalar", lambda e, s=s: e.activation(s[:, 5:6], s[:, 5:6], AF.Exp, scale=-0.5),
               inc=sems["p5rs"])
            do(p, "vector",
               lambda e, par=par, s=s: e.tensor_scalar(ybuf[par][:, :], ybuf[par][:, :],
                                                       s[:, 2:3], s[:, 5:6], ALU.add, ALU.mult),
               waits=[(sems["p5rs"], tc + 1)])
            do(p, "vector",
               lambda e, par=par: e.tensor_tensor(ybuf[par][:, :], ybuf[par][:, :], x2rb[par][:, :], ALU.add),
               inc=sems["p5yb"],
               waits=[(p5xl[par], 16 * (tc // 2 + 1))])
            do(p, "sync", lambda e, tc=tc, par=par: e.dma_start(out[tc * 128:(tc + 1) * 128, :],
                                                                ybuf[par][:, :]),
               inc=sems["p5out"], inc_k=16, waits=[(sems["p5yb"], sems["p5yb"].n)])
        waitonly(p, "sync", [(sems["p5out"], 16 * NT)])

        run_phase(p)
        st5.close()

    return nc


_NC_CACHE = {}


def get_nc(T=2048, D=2048, NH=16):
    key = (T, D, NH)
    if key not in _NC_CACHE:
        _NC_CACHE[key] = build_nc(T, D, NH)
    return _NC_CACHE[key]


def kernel(L0_attn_x, L0_attn_Wq, L0_attn_Wk, L0_attn_Wv, L0_attn_Wo,
           L0_ffn_weight, L0_ffn_bias):
    x = np.ascontiguousarray(np.asarray(L0_attn_x, dtype=np.float32))
    wq = np.ascontiguousarray(np.asarray(L0_attn_Wq, dtype=np.float32))
    wk = np.ascontiguousarray(np.asarray(L0_attn_Wk, dtype=np.float32))
    wv = np.ascontiguousarray(np.asarray(L0_attn_Wv, dtype=np.float32))
    wo = np.ascontiguousarray(np.asarray(L0_attn_Wo, dtype=np.float32))
    wf = np.ascontiguousarray(np.asarray(L0_ffn_weight, dtype=np.float32))
    fb = np.ascontiguousarray(np.asarray(L0_ffn_bias, dtype=np.float32)).reshape(1, -1)

    B = x.shape[0]
    nc = get_nc(x.shape[1], x.shape[2], 16)
    in_maps = [{"x": np.ascontiguousarray(x[b]), "wq": wq, "wk": wk, "wv": wv,
                "wo": wo, "wf": wf, "fbias": fb} for b in range(B)]
    res = run_bass_kernel_spmd(nc, in_maps, core_ids=list(range(B)))
    return np.stack([np.asarray(r["out"], dtype=np.float32) for r in res.results], axis=0)


# version 22
# speedup vs baseline: 1.2084x; 1.2084x over previous
import math
import numpy as np
from contextlib import ExitStack

import concourse.bass as bass
import concourse.mybir as mybir
from concourse.bass_utils import run_bass_kernel_spmd

F32 = mybir.dt.float32
BF16 = mybir.dt.bfloat16
AF = mybir.ActivationFunctionType
ALU = mybir.AluOpType
AXX = mybir.AxisListType.X

EPS = 1e-5
CSHIFT = 5.0


class Sem:
    def __init__(self, nc, stack, name):
        self.h = stack.enter_context(nc.semaphore(name))
        self.n = 0


def build_nc(T=2048, D=2048, NH=16, phases=5, dbg=False, p4lite=0):
    HD = D // NH
    assert HD == 128 and T % 256 == 0 and D % 256 == 0
    NT = T // 128
    ND = D // 128
    TH = T // 2
    DH = D // 2
    SC = 1.0 / math.sqrt(HD)
    PQ = 512

    nc = bass.Bass("TRN2", target_bir_lowering=False, detect_race_conditions=False)

    x_in = nc.dram_tensor("x", [T, D], F32, kind="ExternalInput")
    w_in = {k: nc.dram_tensor(k, [D, D], F32, kind="ExternalInput")
            for k in ("wq", "wk", "wv", "wo", "wf")}
    b_in = nc.dram_tensor("fbias", [1, D], F32, kind="ExternalInput")
    out = nc.dram_tensor("out", [T, D], F32, kind="ExternalOutput")

    KD = "ExternalOutput" if dbg else "Internal"
    s_w = {k: nc.dram_tensor("s_" + k, [D, D], BF16, kind=KD) for k in ("wq", "wk", "wv", "wo")}
    s_qf = nc.dram_tensor("s_qf", [D, D], BF16, kind=KD)
    s_xn1 = nc.dram_tensor("s_xn1", [T, D], BF16, kind=KD)
    s_qt = nc.dram_tensor("s_qt", [D, T], BF16, kind=KD)
    s_kt = nc.dram_tensor("s_kt", [D, T], BF16, kind=KD)
    s_v = nc.dram_tensor("s_v", [T, D], BF16, kind=KD)
    s_woT = nc.dram_tensor("s_woT", [D, D], BF16)
    s_qfT = nc.dram_tensor("s_qfT", [D, D], BF16)
    s_xn2T = nc.dram_tensor("s_xn2T", [D, T], BF16)
    s_x2 = nc.dram_tensor("s_x2", [T, D], F32, kind=KD)
    s_xn2 = nc.dram_tensor("s_xn2", [T, D], BF16, kind=KD)

    poff = []
    acc = 0
    for kc in range(NT):
        poff.append(acc)
        acc += T - kc * 128
    PTOT = acc

    def pieces(lo, hi, base):
        res = []
        p = lo
        while p < hi:
            q = min(hi, base + ((p - base) // PQ + 1) * PQ)
            res.append((p, q))
            p = q
        return res

    with ExitStack() as st0:
        en0 = st0.enter_context
        ones_sb = en0(nc.sbuf_tensor("ones_sb", [128, 128], BF16))
        masktile = en0(nc.sbuf_tensor("masktile", [128, 128], BF16))
        ident = en0(nc.sbuf_tensor("ident", [128, 128], BF16))

        sems = {}
        for name in [
            "p1xl0", "p1xl1", "p1xl2", "p1xl3", "p1sq", "p1var", "p1ap", "p1st", "p1tt", "p1tx",
            "gcast", "p2tr", "p2mm", "p2ev", "p2st",
            "p3hd0", "p3hd1", "p3hv0", "p3hv1", "p3hv2", "p3hv3", "p3s", "p3e", "p3m", "p3l", "p3rc", "p3z", "p3zev",
            "wf0", "wf1", "tq0", "tq1", "terv",
            "p3ws", "p3wt", "p4qs", "p4qt", "p4xs", "p4xt", "p5xs", "p5xt", "p4tr", "p4mm", "p4sq", "p4var", "p4rs", "p4ap", "p4x2", "p4xn",
            "p4xl0", "p4xl1", "p4s2", "p4sn",
            "p5tr", "p5trs", "p5mm", "p5yb", "p5sq", "p5var", "p5rs", "p5ap", "p5out",
            "p5xl0", "p5xl1", "biasld", "vch", "sch", "red",
        ]:
            sems[name] = Sem(nc, st0, "sem_" + name)

        def run_phase(prog):
            with nc.Block() as blk:
                def mk(fns):
                    def body(e):
                        for f in fns:
                            f(e)
                    return body
                blk.sync(mk(prog["sync"]))
                blk.gpsimd(mk(prog["gpsimd"]))
                blk.vector(mk(prog["vector"]))
                blk.scalar(mk(prog["scalar"]))
                blk.tensor(mk(prog["tensor"]))

        def new_prog():
            return {k: [] for k in ("sync", "gpsimd", "vector", "scalar", "tensor")}

        def do(prog, eng, build, inc=None, inc_k=1, waits=(), sink=None):
            w = [(s.h, v) for (s, v) in waits if v > 0]
            ih = None
            if inc is not None:
                inc.n += inc_k
                ih = inc.h
            ik = inc_k

            def f(e, w=w, build=build, ih=ih, ik=ik):
                for (h, v) in w:
                    e.wait_ge(h, v)
                inst = build(e)
                if ih is not None:
                    inst.then_inc(ih, ik)
            (prog[eng] if sink is None else sink).append(f)

        def waitonly(prog, eng, waits):
            w = [(s.h, v) for (s, v) in waits if v > 0]

            def f(e, w=w):
                for (h, v) in w:
                    e.wait_ge(h, v)
            prog[eng].append(f)


        def drain_all(extra=()):
            pd = new_prog()
            ws = [(s, s.n) for s in sems.values() if s.n > 0]
            waitonly(pd, "sync", [(s, v) for (s, v) in ws])
            run_phase(pd)

        st1 = ExitStack()
        en1 = st1.enter_context
        xn1T = en1(nc.sbuf_tensor("xn1T", [128, ND * T], BF16))
        HW0 = D // 2
        wS0 = en1(nc.sbuf_tensor("wS0", [128, ND * HW0], BF16))
        wfb = en1(nc.sbuf_tensor("wfb0", [128, D], F32))
        ter = en1(nc.sbuf_tensor("tter0", [128, D], BF16))
        qb = [en1(nc.sbuf_tensor(f"qb{i}", [128, D], BF16)) for i in range(2)]
        tstat = en1(nc.sbuf_tensor("tstat", [128, 4], F32))
        st1p = ExitStack()
        en1p = st1p.enter_context
        pst = en1p(nc.psum_tensor("pst", [128, 2048], BF16))
        xbuf = [en1p(nc.sbuf_tensor(f"p1x{i}", [128, D], F32)) for i in range(3)]
        xnb = [en1p(nc.sbuf_tensor(f"p1n{i}", [128, D], BF16)) for i in range(2)]
        scrap1 = en1p(nc.sbuf_tensor("p1scrap", [128, D], F32))
        stt1 = [en1p(nc.sbuf_tensor(f"p1s{i}", [128, 8], F32)) for i in range(2)]

        p = new_prog()
        do(p, "gpsimd", lambda e: e.memset(ones_sb[:, :], 1.0))
        do(p, "gpsimd", lambda e: e.memset(masktile[:, :], 1.0))
        do(p, "gpsimd", lambda e: e.affine_select(masktile[:, :], masktile[:, :],
                                                  [[1, 128]], ALU.is_ge, 0.0,
                                                  base=0, channel_multiplier=-1))
        do(p, "gpsimd", lambda e: e.memset(ident[:, :], 1.0))
        do(p, "gpsimd", lambda e: e.affine_select(ident[:, :], ident[:, :],
                                                  [[1, 128]], ALU.is_ge, 0.0,
                                                  base=0, channel_multiplier=-1))
        do(p, "gpsimd", lambda e: e.affine_select(ident[:, :], ident[:, :],
                                                  [[-1, 128]], ALU.is_ge, 0.0,
                                                  base=0, channel_multiplier=1))
        NC4 = max(1, (D * D * 4) // (4 << 20))
        CR = D // NC4
        gc_after = {}
        for wname in ("wq", "wk"):
            for i in range(NC4):
                sl = slice(i * CR, (i + 1) * CR)
                do(p, "gpsimd",
                   lambda e, wn=wname, sl=sl: e.dma_start(s_w[wn][sl, :], w_in[wn][sl, :]),
                   inc=sems["gcast"], inc_k=16)
            gc_after[wname] = sems["gcast"].n

        xl = [sems["p1xl0"], sems["p1xl1"], sems["p1xl2"]]

        def p1_load(tc):
            par = tc % 3
            lw = [(sems["p1ap"], tc - 2)] if tc >= 3 else []
            do(p, "sync",
               lambda e, tc=tc, par=par: e.dma_start(xbuf[par][:, :],
                                                     x_in[tc * 128:(tc + 1) * 128, :]),
               inc=xl[par], inc_k=16, waits=lw)

        def emit_ternary_fullrow():
            terv = sems["terv"]
            vch = sems["vch"]
            wfl = sems["wf0"]
            tqs = [sems["tq0"], sems["tq1"]]
            for oc in range(ND):
                par = oc % 2
                do(p, "gpsimd",
                   lambda e, oc=oc: e.dma_start(wfb[:, :], w_in["wf"][oc * 128:(oc + 1) * 128, :]),
                   inc=wfl, inc_k=16,
                   waits=[(terv, 3 * oc)] if oc >= 1 else [])
                do(p, "vector",
                   lambda e: e.tensor_reduce(
                       tstat[:, 0:1], wfb[:, :], AXX, ALU.add,
                       apply_absolute_value=True),
                   waits=[(wfl, 16 * (oc + 1))], inc=vch)
                do(p, "vector", lambda e: e.tensor_scalar(tstat[:, 1:2], tstat[:, 0:1], 0.7 / D, None, ALU.mult),
                   waits=[(vch, vch.n)], inc=vch)
                do(p, "vector", lambda e: e.tensor_scalar(tstat[:, 2:3], tstat[:, 1:2], -1.0, None, ALU.mult),
                   waits=[(vch, vch.n)], inc=vch)
                waitonly(p, "vector", [(vch, vch.n)])
                do(p, "vector",
                   lambda e: e.tensor_scalar(ter[:, :], wfb[:, :],
                                             tstat[:, 1:2], None, ALU.is_gt),
                   inc=terv)
                do(p, "vector",
                   lambda e, par=par: e.tensor_scalar(qb[par][:, :], wfb[:, :],
                                                      tstat[:, 2:3], None, ALU.is_lt),
                   waits=([(tqs[par], 16 * (oc // 2))] if oc >= 2 else []), inc=terv)
                do(p, "vector",
                   lambda e, par=par: e.tensor_tensor(qb[par][:, :], ter[:, :],
                                                      qb[par][:, :], ALU.subtract),
                   inc=terv)
                do(p, "gpsimd",
                   lambda e, oc=oc, par=par: e.dma_start(
                       s_qf[oc * 128:(oc + 1) * 128, :], qb[par][:, :]),
                   inc=tqs[par], inc_k=16, waits=[(sems["terv"], sems["terv"].n)])

        s0_emitted = [0]

        def slab0_tr(ic):
            first = (s0_emitted[0] == 0)
            s0_emitted[0] += 1
            do(p, "sync",
               lambda e, ic=ic: e.dma_start(
                   wS0[:, ic * HW0:(ic + 1) * HW0],
                   s_w["wq"][0:HW0, ic * 128:(ic + 1) * 128], transpose=True),
               inc=sems["p2tr"], inc_k=16,
               waits=[(sems["gcast"], gc_after["wq"])] if first else ())

        for tc in range(min(3, NT)):
            p1_load(tc)
        for tc in range(NT):
            par = tc % 2
            par4x = tc % 3
            s = stt1[par]
            ldv = 16 * (tc // 3 + 1)
            red, sch = sems["red"], sems["sch"]
            do(p, "vector", lambda e, par4x=par4x, s=s: e.reduce_sum(s[:, 0:1], xbuf[par4x][:, :], axis=AXX),
               waits=[(xl[par4x], ldv), (sems["p1ap"], tc - 1)], inc=red)
            do(p, "scalar", lambda e, par4x=par4x, s=s: e.activation(scrap1[:, :], xbuf[par4x][:, :], AF.Square,
                                                                accum_out=s[:, 1:2]),
               inc=sch, waits=[(xl[par4x], ldv)])
            sqv = sch.n
            do(p, "scalar", lambda e, s=s: e.mul(s[:, 2:3], s[:, 0:1], -1.0 / D),
               waits=[(red, red.n)])
            do(p, "scalar", lambda e, s=s: e.activation(s[:, 3:4], s[:, 2:3], AF.Square))
            do(p, "scalar", lambda e, s=s: e.mul(s[:, 4:5], s[:, 3:4], -1.0), inc=sch)
            negv = sch.n
            do(p, "scalar", lambda e, s=s: e.mul(s[:, 7:8], s[:, 1:2], 1.0 / D),
               waits=[(sch, sqv)])
            do(p, "scalar", lambda e, s=s: e.add(s[:, 7:8], s[:, 7:8], s[:, 4:5]),
               waits=[(sch, negv)])
            do(p, "scalar", lambda e, s=s: e.activation(s[:, 5:6], s[:, 7:8], AF.Ln, bias=np.float32(EPS)))
            do(p, "scalar", lambda e, s=s: e.activation(s[:, 5:6], s[:, 5:6], AF.Exp, scale=-0.5),
               inc=sch)
            do(p, "scalar", lambda e, s=s: e.mul(s[:, 6:7], s[:, 2:3], s[:, 5:6]),
               waits=[(sch, sch.n)], inc=sch)
            aw = [(sems["p1tt"], ND * (tc - 1))] if tc >= 2 else []
            do(p, "scalar", lambda e, par=par, par4x=par4x, s=s: e.activation(xnb[par][:, :], xbuf[par4x][:, :], AF.Identity,
                                                                bias=s[:, 6:7], scale=s[:, 5:6]),
               inc=sems["p1ap"], waits=aw + [(sch, sch.n)])
            for ic in range(ND):
                gidx = tc * ND + ic
                w = [(sems["p1ap"], tc + 1)] if ic == 0 else []
                if gidx >= 2:
                    w.append((sems["p1tx"], gidx - 1))
                do(p, "tensor",
                   lambda e, ic=ic, par=par, gidx=gidx: e.transpose(
                       pst[:, (gidx % 2) * 1024:(gidx % 2) * 1024 + 128],
                       xnb[par][:, ic * 128:(ic + 1) * 128], ident[:, :]),
                   inc=sems["p1tt"], waits=w)
                do(p, "vector",
                   lambda e, tc=tc, ic=ic, gidx=gidx: e.tensor_copy(
                       xn1T[:, ic * T + tc * 128: ic * T + (tc + 1) * 128],
                       pst[:, (gidx % 2) * 1024:(gidx % 2) * 1024 + 128]),
                   inc=sems["p1tx"], waits=[(sems["p1tt"], gidx + 1)])
            if tc + 3 < NT:
                p1_load(tc + 3)
            if 6 <= tc and (tc - 6) < ND:
                slab0_tr(tc - 6)
        for ic in range(s0_emitted[0], ND):
            slab0_tr(ic)
        run_phase(p)
        st1p.close()
        p1st_total = sems["p1st"].n
        tq_done = [(sems["tq0"], sems["tq0"].n), (sems["tq1"], sems["tq1"].n)]
        if phases == 1:
            drain_all()
            st1.close()
            return nc

        HW = D // 2
        wS = [wS0] + [en1(nc.sbuf_tensor(f"wS{i}", [128, ND * HW], BF16)) for i in range(1, 3)]
        gob = [en1(nc.sbuf_tensor(f"gob{i}", [128, D], BF16)) for i in range(2)]
        psA = en1(nc.psum_tensor("psA", [128, max(T, D)], F32))
        psB = en1(nc.psum_tensor("psB", [128, max(T, D)], F32))
        psP2 = [psA, psB]

        p = new_prog()
        for wname in ("wv",):
            for i in range(NC4):
                sl = slice(i * CR, (i + 1) * CR)
                do(p, "gpsimd",
                   lambda e, wn=wname, sl=sl: e.dma_start(s_w[wn][sl, :], w_in[wn][sl, :]),
                   inc=sems["gcast"], inc_k=16)
            gc_after[wname] = sems["gcast"].n
        emit_ternary_fullrow()
        mmcnt, evcnt, stcnt = sems["p2mm"], sems["p2ev"], sems["p2st"]
        store_done = []
        unit_done = []

        p1tx_total = sems["p1tx"].n
        UNITS = [("wq", 0, "qk"), ("wk", 0, "qk"), ("wv", 0, "v"),
                 ("wq", 1, "qk"), ("wk", 1, "qk"), ("wv", 1, "v")]
        tr_need = []

        def emit_slab(u):
            wname, oh, _kind = UNITS[u]
            slab = u % 3
            for ic in range(ND):
                w = []
                if ic == 0:
                    w.append((sems["gcast"], gc_after[wname]))
                    if u >= 3:
                        w.append((mmcnt, unit_done[u - 3]))
                do(p, "sync",
                   lambda e, wn=wname, oh=oh, ic=ic, slab=slab: e.dma_start(
                       wS[slab][:, ic * HW:(ic + 1) * HW],
                       s_w[wn][oh * HW:(oh + 1) * HW, ic * 128:(ic + 1) * 128],
                       transpose=True),
                   inc=sems["p2tr"], inc_k=16, waits=w)
            tr_need.append(sems["p2tr"].n)

        def unit_gemm(u):
            wname, oh, kind = UNITS[u]
            slab = u % 3
            dst = {"wq": s_qt, "wk": s_kt, "wv": s_v}[wname]
            if kind == "qk":
                nq4 = max(1, T // PQ)
                pc4 = min(PQ, T)
                for oc8 in range(HW // 128):
                    oc = oh * (HW // 128) + oc8
                    E = evcnt.n
                    base_mm = mmcnt.n
                    for ic in range(ND):
                        for tq4 in range(nq4):
                            w = []
                            if oc8 == 0 and tq4 == 0 and ic == 0:
                                w.append((sems["p2tr"], tr_need[u]))
                                w.append((sems["p1tx"], p1tx_total))
                            if tq4 == 0 and ic == 0 and E >= 2:
                                w.append((evcnt, E - 1))
                            last = (tq4 == nq4 - 1 and ic == ND - 1)
                            do(p, "tensor",
                               lambda e, oc8=oc8, tq4=tq4, ic=ic, slab=slab, E=E, pc4=pc4: e.matmul(
                                   psP2[E % 2][:, tq4 * pc4:(tq4 + 1) * pc4],
                                   wS[slab][:, ic * HW + oc8 * 128: ic * HW + (oc8 + 1) * 128],
                                   xn1T[:, ic * T + tq4 * pc4: ic * T + (tq4 + 1) * pc4],
                                   start=(ic == 0), stop=(ic == ND - 1)),
                               inc=(mmcnt if last else None), waits=w)
                    ew = [(mmcnt, mmcnt.n)]
                    if E >= 2:
                        ew.append((stcnt, store_done[E - 2]))
                    do(p, "scalar",
                       lambda e, E=E: e.copy(gob[E % 2][:, 0:T], psP2[E % 2][:, 0:T]),
                       inc=evcnt, waits=ew)
                    do(p, "scalar",
                       lambda e, oc=oc, E=E: e.dma_start(
                           dst[oc * 128:(oc + 1) * 128, :], gob[E % 2][:, 0:T]),
                       inc=stcnt, inc_k=16, waits=[(evcnt, evcnt.n)])
                    store_done.append(stcnt.n)
            else:
                nv2 = max(1, HW // PQ)
                pcv = min(PQ, HW)
                for tc in range(NT):
                    E = evcnt.n
                    for ic in range(ND):
                        for ov2 in range(nv2):
                            w = []
                            if tc == 0 and ic == 0 and ov2 == 0:
                                w.append((sems["p2tr"], tr_need[u]))
                                w.append((sems["p1tx"], p1tx_total))
                            if ic == 0 and ov2 == 0 and E >= 2:
                                w.append((evcnt, E - 1))
                            last = (ic == ND - 1 and ov2 == nv2 - 1)
                            do(p, "tensor",
                               lambda e, tc=tc, ic=ic, ov2=ov2, slab=slab, E=E, pcv=pcv: e.matmul(
                                   psP2[E % 2][:, ov2 * pcv:(ov2 + 1) * pcv],
                                   xn1T[:, ic * T + tc * 128: ic * T + (tc + 1) * 128],
                                   wS[slab][:, ic * HW + ov2 * pcv: ic * HW + (ov2 + 1) * pcv],
                                   start=(ic == 0), stop=(ic == ND - 1)),
                               inc=(mmcnt if last else None), waits=w)
                    ew = [(mmcnt, mmcnt.n)]
                    if E >= 2:
                        ew.append((stcnt, store_done[E - 2]))
                    do(p, "scalar",
                       lambda e, E=E: e.copy(gob[E % 2][:, 0:HW], psP2[E % 2][:, 0:HW]),
                       inc=evcnt, waits=ew)
                    do(p, "scalar",
                       lambda e, tc=tc, oh=oh, E=E: e.dma_start(
                           s_v[tc * 128:(tc + 1) * 128, oh * HW:(oh + 1) * HW],
                           gob[E % 2][:, 0:HW]),
                       inc=stcnt, inc_k=16, waits=[(evcnt, evcnt.n)])
                    store_done.append(stcnt.n)
            unit_done.append(mmcnt.n)

        tr_need.append(sems["p2tr"].n)
        emit_slab(1)
        emit_slab(2)
        for u in range(len(UNITS)):
            unit_gemm(u)
            if u + 3 < len(UNITS):
                emit_slab(u + 3)

        for i in range(NC4):
            sl = slice(i * CR, (i + 1) * CR)
            do(p, "gpsimd", lambda e, sl=sl: e.dma_start(s_w["wo"][sl, :], w_in["wo"][sl, :]),
               inc=sems["gcast"], inc_k=16)
        gc_wo = sems["gcast"].n

        run_phase(p)
        st1.close()
        p2st_total = stcnt.n
        if phases == 2:
            drain_all()
            return nc

        st2 = ExitStack()
        zT = st2.enter_context(nc.sbuf_tensor("zT", [128, NH * T], BF16))
        stg = st2.enter_context(nc.sbuf_tensor("stg", [128, D], BF16))
        st3 = ExitStack()
        en3 = st3.enter_context
        qt_h = [en3(nc.sbuf_tensor(f"qth{i}", [128, T], BF16)) for i in range(2)]
        kt_h = [en3(nc.sbuf_tensor(f"kth{i}", [128, T], BF16)) for i in range(2)]
        v_h = [en3(nc.sbuf_tensor(f"vh{i}", [128, NT * 128], BF16)) for i in range(4)]
        put = [en3(nc.sbuf_tensor(f"put{i}", [128, PTOT], BF16)) for i in range(2)]
        rcl = [en3(nc.sbuf_tensor(f"rcl{i}", [128, T], F32)) for i in range(2)]
        ps_s = en3(nc.psum_tensor("ps_s", [128, 3072], F32))
        ps_lz = en3(nc.psum_tensor("ps_lz", [128, 1024], F32))


        p = new_prog()
        hd_sem = [sems["p3hd0"], sems["p3hd1"]]
        hv_sem = [sems["p3hv0"], sems["p3hv1"], sems["p3hv2"], sems["p3hv3"]]
        schunks = 0
        slot_req = {}
        lz_free = {}
        lz_cursor = [0]
        NQG = (T + 511) // 512
        prev_lz = []
        prev_v = []

        def riffle(a, b):
            out_ = []
            ia = ib = 0
            while ia < len(a) or ib < len(b):
                while ib < len(b) and (ia >= len(a) or ib * max(1, len(a)) < ia * len(b)):
                    out_.append(b[ib]); ib += 1
                if ia < len(a):
                    out_.append(a[ia]); ia += 1
            return out_

        for h in range(NH):
            par = h % 2
            s_list = []
            lz_list = []
            v_list = []
            par4 = h % 4
            lw = [(stcnt, p2st_total)] if h == 0 else []
            if h >= 2:
                lw.append((sems["p3zev"], NQG * (h - 1)))
            do(p, "sync", lambda e, h=h, par=par: e.dma_start(qt_h[par][:, :], s_qt[h * 128:(h + 1) * 128, :]),
               inc=hd_sem[par], inc_k=16, waits=lw)
            do(p, "sync", lambda e, h=h, par=par: e.dma_start(kt_h[par][:, :], s_kt[h * 128:(h + 1) * 128, :]),
               inc=hd_sem[par], inc_k=16)
            vw4 = [(sems["p3zev"], NQG * (h - 3))] if h >= 4 else []
            for kc in range(NT):
                do(p, "sync", lambda e, h=h, par4=par4, kc=kc: e.dma_start(
                    v_h[par4][:, kc * 128:(kc + 1) * 128],
                    s_v[kc * 128:(kc + 1) * 128, h * 128:(h + 1) * 128]),
                   inc=hv_sem[par4], inc_k=16, waits=vw4 if kc == 0 else ())
            ld_need = 32 * (h // 2 + 1)
            vld_need = NT * 16 * (h // 4 + 1)
            if h < ND:
                do(p, "sync",
                   lambda e, ic=h: e.dma_start(stg[:, :], s_w["wo"][:, ic * 128:(ic + 1) * 128],
                                               transpose=True),
                   inc=sems["p3ws"], inc_k=16,
                   waits=[(sems["gcast"], gc_wo), (sems["p3wt"], 16 * h)])
                do(p, "sync",
                   lambda e, ic=h: e.dma_start(s_woT[ic * 128:(ic + 1) * 128, :], stg[:, :]),
                   inc=sems["p3wt"], inc_k=16, waits=[(sems["p3ws"], 16 * (h + 1))])

            for kc in range(NT):
                chs = pieces(kc * 128, T, kc * 128)
                if schunks % 2:
                    schunks += 1
                j = 0
                while j < len(chs):
                    pair = chs[j:j + 2]
                    s0 = schunks % 6
                    for (a, b) in pair:
                        slot = schunks % 6
                        w = []
                        if kc == 0 and a == chs[0][0]:
                            w.append((hd_sem[par], ld_need))
                        req = slot_req.get(slot, 0)
                        if req > 0:
                            w.append((sems["p3e"], req))
                        do(p, "tensor",
                           lambda e, kc=kc, a=a, b=b, slot=slot, par=par: e.matmul(
                               ps_s[:, slot * 512: slot * 512 + (b - a)],
                               kt_h[par][:, kc * 128:(kc + 1) * 128],
                               qt_h[par][:, a:b], start=True, stop=True),
                           inc=sems["p3s"], waits=w, sink=s_list)
                        schunks += 1
                    a0, b1 = pair[0][0], pair[-1][1]
                    ew = [(sems["p3s"], sems["p3s"].n)]
                    if h >= 2 and kc == 0 and j == 0:
                        ew.append((sems["p3zev"], NQG * (h - 1)))
                    do(p, "scalar",
                       lambda e, kc=kc, a0=a0, b1=b1, s0=s0, par=par: e.activation(
                           put[par][:, poff[kc] + (a0 - kc * 128): poff[kc] + (b1 - kc * 128)],
                           ps_s[:, s0 * 512: s0 * 512 + (b1 - a0)],
                           AF.Exp, scale=SC, bias=np.float32(-CSHIFT)),
                       inc=sems["p3e"], waits=ew)
                    for (a, b) in pair:
                        slot_req[s0] = sems["p3e"].n
                        s0 = (s0 + 1) % 6
                    j += 2
                do(p, "gpsimd",
                   lambda e, kc=kc, par=par: e.affine_select(
                       put[par][:, poff[kc]:poff[kc] + 128],
                       put[par][:, poff[kc]:poff[kc] + 128],
                       [[1, 128]], ALU.is_ge, 0.0, base=0, channel_multiplier=-1),
                   inc=sems["p3m"], waits=[(sems["p3e"], sems["p3e"].n)])

            NQ = (T + 511) // 512
            for q in range(NQ):
                qlo, qhi = q * 512, min((q + 1) * 512, T)
                gq = h * NQ + q
                slot = lz_cursor[0] % 2 * 512
                kcs_q = [kc for kc in range(NT) if kc * 128 < qhi]
                segs = []
                for kc in kcs_q:
                    lo = max(qlo, kc * 128)
                    cut = min(max((kc + 1) * 128, lo), qhi)
                    for (lo2, hi2, stp) in ((lo, cut, True), (cut, qhi, False)):
                        for (a, b) in pieces(lo2, hi2, qlo):
                            segs.append((kc, a, b, stp))
                kmax_q = min((qhi + 127) // 128, NT)
                for idx, (kc, a, b, stp) in enumerate(segs):
                    w = []
                    if idx == 0:
                        w.append((sems["p3m"], NT * h + kmax_q))
                        fr = lz_free.get(lz_cursor[0] % 2)
                        if fr is not None:
                            w.append(fr)
                    do(p, "tensor",
                       lambda e, kc=kc, a=a, b=b, par=par, qlo=qlo, slot=slot, stp=stp: e.matmul(
                           ps_lz[:, slot + (a - qlo): slot + (b - qlo)], ones_sb[:, :],
                           put[par][:, poff[kc] + (a - kc * 128): poff[kc] + (b - kc * 128)],
                           start=(kc == 0), stop=stp, skip_group_check=True),
                       inc=(sems["p3l"] if idx == len(segs) - 1 else None), waits=w, sink=lz_list)
                do(p, "vector",
                   lambda e, par=par, qlo=qlo, qhi=qhi, slot=slot: e.reciprocal(
                       rcl[par][:, qlo:qhi], ps_lz[:, slot: slot + (qhi - qlo)]),
                   inc=sems["p3rc"], waits=[(sems["p3l"], gq + 1)], sink=v_list)
                lz_free[lz_cursor[0] % 2] = (sems["p3rc"], sems["p3rc"].n)
                lz_cursor[0] += 1

            for q in range(NQ):
                qlo, qhi = q * 512, min((q + 1) * 512, T)
                gq = h * NQ + q
                slot = lz_cursor[0] % 2 * 512
                kcs_q = [kc for kc in range(NT) if kc * 128 < qhi]
                segs = []
                for kc in kcs_q:
                    lo = max(qlo, kc * 128)
                    cut = min(max((kc + 1) * 128, lo), qhi)
                    for (lo2, hi2, stp) in ((lo, cut, True), (cut, qhi, False)):
                        for (a, b) in pieces(lo2, hi2, qlo):
                            segs.append((kc, a, b, stp))
                kmax_q = min((qhi + 127) // 128, NT)
                for idx, (kc, a, b, stp) in enumerate(segs):
                    w = []
                    if idx == 0:
                        w.append((sems["p3m"], NT * h + kmax_q))
                        if q == 0:
                            w.append((hv_sem[par4], vld_need))
                        fr = lz_free.get(lz_cursor[0] % 2)
                        if fr is not None:
                            w.append(fr)
                    do(p, "tensor",
                       lambda e, kc=kc, a=a, b=b, par=par, par4=par4, qlo=qlo, slot=slot, stp=stp: e.matmul(
                           ps_lz[:, slot + (a - qlo): slot + (b - qlo)],
                           v_h[par4][:, kc * 128:(kc + 1) * 128],
                           put[par][:, poff[kc] + (a - kc * 128): poff[kc] + (b - kc * 128)],
                           start=(kc == 0), stop=stp, skip_group_check=True),
                       inc=(sems["p3z"] if idx == len(segs) - 1 else None), waits=w, sink=lz_list)
                do(p, "vector",
                   lambda e, h=h, par=par, qlo=qlo, qhi=qhi, slot=slot: e.tensor_tensor(
                       zT[:, h * T + qlo: h * T + qhi],
                       ps_lz[:, slot: slot + (qhi - qlo)], rcl[par][:, qlo:qhi], ALU.mult),
                   inc=sems["p3zev"], waits=[(sems["p3z"], gq + 1)], sink=v_list)
                lz_free[lz_cursor[0] % 2] = (sems["p3zev"], sems["p3zev"].n)
                lz_cursor[0] += 1

            p["tensor"] += riffle(s_list, prev_lz)
            prev_lz = lz_list
            p["vector"] += prev_v
            prev_v = v_list
        p["tensor"] += prev_lz
        p["vector"] += prev_v

        run_phase(p)
        st3.close()
        tq_tot = [sems["tq0"].n, sems["tq1"].n]
        if phases == 3:
            drain_all()
            st2.close()
            return nc

        st4 = ExitStack()
        en4 = st4.enter_context
        woT = en4(nc.sbuf_tensor("woT", [128, ND * D], BF16))
        xrb = [en4(nc.sbuf_tensor(f"xrb{i}", [128, D], F32)) for i in range(2)]
        scrap4 = en4(nc.sbuf_tensor("scrap4", [128, D], F32))
        x2b = en4(nc.sbuf_tensor("x2b", [128, D], F32))
        xn2b = [en4(nc.sbuf_tensor(f"xn2b{i}", [128, D], BF16)) for i in range(2)]
        stt4 = [en4(nc.sbuf_tensor(f"p4s{i}", [128, 8], F32)) for i in range(2)]
        ps4 = [en4(nc.psum_tensor(f"ps4_{i}", [128, D], F32)) for i in range(2)]

        p = new_prog()
        p3wt_total = sems["p3wt"].n
        for ic in range(ND):
            do(p, "sync",
               lambda e, ic=ic: e.dma_start(woT[:, ic * D:(ic + 1) * D],
                                            s_woT[ic * 128:(ic + 1) * 128, :]),
               inc=sems["p4tr"], inc_k=16,
               waits=[(sems["p3wt"], p3wt_total)] if ic == 0 else ())
        for ic in range(ND):
            do(p, "sync",
               lambda e, ic=ic: e.dma_start(stg[:, :], s_qf[:, ic * 128:(ic + 1) * 128],
                                            transpose=True),
               inc=sems["p4qs"], inc_k=16,
               waits=([(sems["tq0"], tq_tot[0]), (sems["tq1"], tq_tot[1])] if ic == 0 else [])
                     + [(sems["p4qt"], 16 * ic)])
            do(p, "sync",
               lambda e, ic=ic: e.dma_start(s_qfT[ic * 128:(ic + 1) * 128, :], stg[:, :]),
               inc=sems["p4qt"], inc_k=16, waits=[(sems["p4qs"], 16 * (ic + 1))])
        for ic in range(ND):
            do(p, "sync",
               lambda e, ic=ic: e.dma_start(stg[:, 0:TH],
                                            s_xn2[0:TH, ic * 128:(ic + 1) * 128], transpose=True),
               inc=sems["p4xs"], inc_k=16,
               waits=([(sems["p4sn"], 16 * (TH // 128)), (sems["p4qt"], 16 * ND)]
                      if ic == 0 else [])
                     + [(sems["p4xt"], 16 * ic)])
            do(p, "sync",
               lambda e, ic=ic: e.dma_start(s_xn2T[ic * 128:(ic + 1) * 128, 0:TH], stg[:, 0:TH]),
               inc=sems["p4xt"], inc_k=16, waits=[(sems["p4xs"], 16 * (ic + 1))])
        p4xl = [sems["p4xl0"], sems["p4xl1"]]

        def p4_load(tc):
            par = tc % 2
            w = [(sems["p4x2"], tc - 1)] if tc >= 2 else []
            do(p, "gpsimd",
               lambda e, tc=tc, par=par: e.dma_start(xrb[par][:, :], x_in[tc * 128:(tc + 1) * 128, :]),
               inc=p4xl[par], inc_k=16, waits=w)

        p4_load(0)
        if NT > 1:
            p4_load(1)
        nq = max(1, D // PQ)
        pc = min(PQ, D)
        red4_base = sems["red"].n
        if p4lite in (10, 11):
            for tc in range(NT):
                par = tc % 2
                s = stt4[par]
                w = [(sems["p4tr"], 16 * ND)]
                if tc >= 2:
                    w.append((sems["p4ap"], tc - 1))
                for ic in range(ND):
                    for oq in range(nq):
                        last = (ic == ND - 1 and oq == nq - 1)
                        do(p, "tensor",
                           lambda e, tc=tc, par=par, ic=ic, oq=oq: e.matmul(
                               ps4[par][:, oq * pc:(oq + 1) * pc],
                               zT[:, ic * T + tc * 128: ic * T + (tc + 1) * 128],
                               woT[:, ic * D + oq * pc: ic * D + (oq + 1) * pc],
                               start=(ic == 0), stop=(ic == ND - 1)),
                           inc=(sems["p4mm"] if last else None),
                           waits=(w if (ic == 0 and oq == 0) else ()))
                vch = sems["vch"]
                sch = sems["sch"]
                if p4lite == 11:
                    vw = [(sems["p4mm"], tc + 1)]
                    if tc >= 2:
                        vw.append((sems["p4ap"], tc - 1))
                    do(p, "vector", lambda e, par=par, s=s: e.reduce_sum(s[:, 0:1], ps4[par][:, :], axis=AXX),
                       waits=vw, inc=vch)
                    do(p, "vector", lambda e, s=s: e.tensor_scalar(s[:, 2:3], s[:, 0:1], -1.0 / D, None, ALU.mult),
                       waits=[(vch, vch.n)], inc=vch)
                    do(p, "vector", lambda e, s=s: e.tensor_tensor(s[:, 3:4], s[:, 2:3], s[:, 2:3], ALU.mult),
                       waits=[(vch, vch.n)], inc=vch)
                do(p, "scalar", lambda e, par=par, s=s: e.activation(scrap4[:, :], ps4[par][:, :], AF.Square,
                                                                    accum_out=s[:, 1:2]),
                   inc=sems["p4sq"], waits=[(sems["p4mm"], tc + 1)])
                if p4lite == 11:
                    do(p, "vector", lambda e, s=s: e.tensor_scalar(s[:, 4:5], s[:, 1:2], 1.0 / D, s[:, 3:4],
                                                                   ALU.mult, ALU.subtract),
                       inc=sems["p4var"], waits=[(sems["p4sq"], tc + 1), (vch, vch.n)])
                    do(p, "scalar", lambda e, s=s: e.activation(s[:, 5:6], s[:, 4:5], AF.Ln, bias=np.float32(EPS)),
                       waits=[(sems["p4var"], tc + 1)])
                else:
                    do(p, "scalar", lambda e, s=s: e.activation(s[:, 5:6], s[:, 1:2], AF.Ln, bias=np.float32(1.0)),
                       waits=[(sems["p4sq"], tc + 1)])
                do(p, "scalar", lambda e, s=s: e.activation(s[:, 5:6], s[:, 5:6], AF.Exp, scale=-0.5),
                   inc=sch)
                do(p, "scalar", lambda e, s=s: e.mul(s[:, 6:7], s[:, 2:3] if p4lite == 11 else s[:, 5:6], s[:, 5:6]),
                   waits=[(sch, sch.n)], inc=sch)
                do(p, "scalar", lambda e, par=par, s=s: e.activation(scrap4[:, :], ps4[par][:, :], AF.Identity,
                                                                    bias=s[:, 6:7], scale=s[:, 5:6]),
                   inc=sems["p4ap"], waits=[(sch, sch.n)])
            run_phase(p)
            drain_all()
            st4.close()
            st2.close()
            return nc
        if p4lite in (8, 9):
            for tc in range(NT):
                par = tc % 2
                s = stt4[par]
                w = [(sems["p4tr"], 16 * ND)]
                if tc >= 2:
                    w.append((sems["p4ap"], tc - 1))
                for ic in range(ND):
                    for oq in range(nq):
                        last = (ic == ND - 1 and oq == nq - 1)
                        do(p, "tensor",
                           lambda e, tc=tc, par=par, ic=ic, oq=oq: e.matmul(
                               ps4[par][:, oq * pc:(oq + 1) * pc],
                               zT[:, ic * T + tc * 128: ic * T + (tc + 1) * 128],
                               woT[:, ic * D + oq * pc: ic * D + (oq + 1) * pc],
                               start=(ic == 0), stop=(ic == ND - 1)),
                           inc=(sems["p4mm"] if last else None),
                           waits=(w if (ic == 0 and oq == 0) else ()))
                if p4lite == 8:
                    do(p, "vector", lambda e, par=par, s=s: e.reduce_sum(s[:, 0:1], ps4[par][:, :], axis=AXX),
                       waits=[(sems["p4mm"], tc + 1)], inc=sems["p4ap"])
                else:
                    do(p, "scalar", lambda e, par=par, s=s: e.activation(scrap4[:, :], ps4[par][:, :], AF.Square,
                                                                        accum_out=s[:, 1:2]),
                       inc=sems["p4ap"], waits=[(sems["p4mm"], tc + 1)])
            run_phase(p)
            drain_all()
            st4.close()
            st2.close()
            return nc
        if p4lite == 6:
            for tc in range(NT):
                par = tc % 2
                s = stt4[par]
                w = [(sems["p4tr"], 16 * ND)]
                if tc >= 2:
                    w.append((sems["p4ap"], tc - 1))
                for ic in range(ND):
                    for oq in range(nq):
                        last = (ic == ND - 1 and oq == nq - 1)
                        do(p, "tensor",
                           lambda e, tc=tc, par=par, ic=ic, oq=oq: e.matmul(
                               ps4[par][:, oq * pc:(oq + 1) * pc],
                               zT[:, ic * T + tc * 128: ic * T + (tc + 1) * 128],
                               woT[:, ic * D + oq * pc: ic * D + (oq + 1) * pc],
                               start=(ic == 0), stop=(ic == ND - 1)),
                           inc=(sems["p4mm"] if last else None),
                           waits=(w if (ic == 0 and oq == 0) else ()))
                vch = sems["vch"]
                vw = [(sems["p4mm"], tc + 1)]
                if tc >= 2:
                    vw.append((sems["p4ap"], tc - 1))
                do(p, "vector", lambda e, par=par, s=s: e.reduce_sum(s[:, 0:1], ps4[par][:, :], axis=AXX),
                   waits=vw, inc=vch)
                do(p, "vector", lambda e, s=s: e.tensor_scalar(s[:, 2:3], s[:, 0:1], -1.0 / D, None, ALU.mult),
                   waits=[(vch, vch.n)], inc=vch)
                do(p, "vector", lambda e, s=s: e.tensor_tensor(s[:, 3:4], s[:, 2:3], s[:, 2:3], ALU.mult),
                   waits=[(vch, vch.n)], inc=vch)
                do(p, "scalar", lambda e, par=par, s=s: e.activation(scrap4[:, :], ps4[par][:, :], AF.Square,
                                                                    accum_out=s[:, 1:2]),
                   inc=sems["p4sq"], waits=[(sems["p4mm"], tc + 1)])
                do(p, "vector", lambda e, s=s: e.tensor_scalar(s[:, 4:5], s[:, 1:2], 1.0 / D, s[:, 3:4],
                                                               ALU.mult, ALU.subtract),
                   inc=sems["p4var"], waits=[(sems["p4sq"], tc + 1), (vch, vch.n)])
                do(p, "scalar", lambda e, s=s: e.activation(s[:, 5:6], s[:, 4:5], AF.Ln, bias=np.float32(EPS)),
                   waits=[(sems["p4var"], tc + 1)])
                sch = sems["sch"]
                do(p, "scalar", lambda e, s=s: e.activation(s[:, 5:6], s[:, 5:6], AF.Exp, scale=-0.5),
                   inc=sch)
                do(p, "scalar", lambda e, s=s: e.mul(s[:, 6:7], s[:, 2:3], s[:, 5:6]),
                   waits=[(sch, sch.n)], inc=sch)
                aw = [(sch, sch.n)]
                if tc >= 1:
                    aw += [(sems["p4var"], tc), (sems["p4s2"], 16 * tc)]
                do(p, "scalar", lambda e, par=par, s=s: e.activation(x2b[:, :], ps4[par][:, :], AF.Identity,
                                                                    bias=s[:, 6:7], scale=s[:, 5:6]),
                   inc=sems["p4ap"], waits=aw)
                do(p, "vector", lambda e, par=par: e.tensor_tensor(x2b[:, :], x2b[:, :], xrb[par][:, :], ALU.add),
                   inc=sems["p4x2"],
                   waits=[(sems["p4ap"], tc + 1), (p4xl[par], 16 * (tc // 2 + 1))])
                do(p, "gpsimd", lambda e, tc=tc: e.dma_start(s_x2[tc * 128:(tc + 1) * 128, :], x2b[:, :]),
                   inc=sems["p4s2"], inc_k=16, waits=[(sems["p4x2"], tc + 1)])
                if tc + 2 < NT:
                    p4_load(tc + 2)
            run_phase(p)
            drain_all()
            st4.close()
            st2.close()
            return nc
        if p4lite in (3, 4, 5):
            run_phase(p)
            drain_all()
            st4.close()
            st2.close()
            return nc
        if p4lite:
            for tc in range(NT):
                par = tc % 2
                w = [(sems["p4tr"], 16 * ND)]
                if tc >= 2:
                    w.append((sems["p4ap"], tc - 1))
                if p4lite == 2:
                    for ic in range(ND):
                        for oq in range(nq):
                            last = (ic == ND - 1 and oq == nq - 1)
                            do(p, "tensor",
                               lambda e, tc=tc, par=par, ic=ic, oq=oq: e.matmul(
                                   ps4[par][:, oq * pc:(oq + 1) * pc],
                                   woT[:, ic * D + tc * 128: ic * D + tc * 128 + 128],
                                   woT[:, ic * D + oq * pc: ic * D + (oq + 1) * pc],
                                   start=(ic == 0), stop=(ic == ND - 1)),
                               inc=(sems["p4mm"] if last else None),
                               waits=(w if (ic == 0 and oq == 0) else ()))
                else:
                    for ic in range(ND):
                        for oq in range(nq):
                            last = (ic == ND - 1 and oq == nq - 1)
                            do(p, "tensor",
                               lambda e, tc=tc, par=par, ic=ic, oq=oq: e.matmul(
                                   ps4[par][:, oq * pc:(oq + 1) * pc],
                                   zT[:, ic * T + tc * 128: ic * T + (tc + 1) * 128],
                                   woT[:, ic * D + oq * pc: ic * D + (oq + 1) * pc],
                                   start=(ic == 0), stop=(ic == ND - 1)),
                               inc=(sems["p4mm"] if last else None),
                               waits=(w if (ic == 0 and oq == 0) else ()))
                do(p, "scalar", lambda e, par=par: e.copy(scrap4[:, :], ps4[par][:, :]),
                   inc=sems["p4ap"], waits=[(sems["p4mm"], tc + 1)])
            run_phase(p)
            drain_all()
            st4.close()
            st2.close()
            return nc
        for tc in range(NT):
            par = tc % 2
            s = stt4[par]
            w = [(sems["p4tr"], 16 * ND)]
            if tc >= 2:
                w.append((sems["p4ap"], tc - 1))
            for ic in range(ND):
                for oq in range(nq):
                    last = (ic == ND - 1 and oq == nq - 1)
                    do(p, "tensor",
                       lambda e, tc=tc, par=par, ic=ic, oq=oq: e.matmul(
                           ps4[par][:, oq * pc:(oq + 1) * pc],
                           zT[:, ic * T + tc * 128: ic * T + (tc + 1) * 128],
                           woT[:, ic * D + oq * pc: ic * D + (oq + 1) * pc],
                           start=(ic == 0), stop=(ic == ND - 1)),
                       inc=(sems["p4mm"] if last else None),
                       waits=(w if (ic == 0 and oq == 0) else ()))
            vw = [(sems["p4mm"], tc + 1)]
            if tc >= 2:
                vw.append((sems["p4xn"], tc - 1))
            red, sch = sems["red"], sems["sch"]
            do(p, "vector", lambda e, par=par, s=s: e.reduce_sum(s[:, 0:1], ps4[par][:, :], axis=AXX),
               waits=vw, inc=red)
            do(p, "scalar", lambda e, par=par, s=s: e.activation(scrap4[:, :], ps4[par][:, :], AF.Square,
                                                                accum_out=s[:, 1:2]),
               inc=sch, waits=[(sems["p4mm"], tc + 1), (red, red4_base + tc + 1),
                               (sems["p4xn"], tc - 1)])
            sqv = sch.n
            do(p, "scalar", lambda e, s=s: e.mul(s[:, 2:3], s[:, 0:1], -1.0 / D))
            do(p, "scalar", lambda e, s=s: e.activation(s[:, 3:4], s[:, 2:3], AF.Square))
            do(p, "scalar", lambda e, s=s: e.mul(s[:, 4:5], s[:, 3:4], -1.0), inc=sch)
            negv = sch.n
            do(p, "scalar", lambda e, s=s: e.mul(s[:, 7:8], s[:, 1:2], 1.0 / D),
               waits=[(sch, sqv)])
            do(p, "scalar", lambda e, s=s: e.add(s[:, 7:8], s[:, 7:8], s[:, 4:5]),
               waits=[(sch, negv)])
            do(p, "scalar", lambda e, s=s: e.activation(s[:, 5:6], s[:, 7:8], AF.Ln, bias=np.float32(EPS)))
            do(p, "scalar", lambda e, s=s: e.activation(s[:, 5:6], s[:, 5:6], AF.Exp, scale=-0.5),
               inc=sch)
            do(p, "scalar", lambda e, s=s: e.mul(s[:, 6:7], s[:, 2:3], s[:, 5:6]),
               waits=[(sch, sch.n)], inc=sch)
            aw = [(sch, sch.n)]
            if tc >= 1:
                aw += [(sems["p4xn"], tc), (sems["p4s2"], 16 * tc)]
            do(p, "scalar", lambda e, par=par, s=s: e.activation(x2b[:, :], ps4[par][:, :], AF.Identity,
                                                                bias=s[:, 6:7], scale=s[:, 5:6]),
               inc=sems["p4ap"], waits=aw)
            do(p, "vector", lambda e, par=par: e.tensor_tensor(x2b[:, :], x2b[:, :], xrb[par][:, :], ALU.add),
               inc=sems["p4x2"],
               waits=[(sems["p4ap"], tc + 1), (p4xl[par], 16 * (tc // 2 + 1))])
            do(p, "gpsimd", lambda e, tc=tc: e.dma_start(s_x2[tc * 128:(tc + 1) * 128, :], x2b[:, :]),
               inc=sems["p4s2"], inc_k=16, waits=[(sems["p4x2"], tc + 1)])
            vch = sems["vch"]
            do(p, "vector", lambda e, s=s: e.reduce_sum(s[:, 0:1], x2b[:, :], axis=AXX),
               waits=[(sems["p4x2"], tc + 1)], inc=vch)
            do(p, "scalar", lambda e, s=s: e.activation(scrap4[:, :], x2b[:, :], AF.Square,
                                                        accum_out=s[:, 1:2]),
               inc=sems["p4sq"], waits=[(sems["p4x2"], tc + 1)])
            do(p, "vector", lambda e, s=s: e.tensor_scalar(s[:, 2:3], s[:, 0:1], -1.0 / D, None, ALU.mult),
               waits=[(vch, vch.n)], inc=vch)
            do(p, "vector", lambda e, s=s: e.tensor_tensor(s[:, 3:4], s[:, 2:3], s[:, 2:3], ALU.mult),
               waits=[(vch, vch.n)], inc=vch)
            do(p, "vector", lambda e, s=s: e.tensor_scalar(s[:, 4:5], s[:, 1:2], 1.0 / D, s[:, 3:4],
                                                           ALU.mult, ALU.subtract),
               waits=[(vch, vch.n), (sems["p4sq"], sems["p4sq"].n)], inc=sems["p4var"])
            do(p, "scalar", lambda e, s=s: e.activation(s[:, 5:6], s[:, 4:5], AF.Ln, bias=np.float32(EPS)),
               waits=[(sems["p4var"], sems["p4var"].n)])
            do(p, "scalar", lambda e, s=s: e.activation(s[:, 5:6], s[:, 5:6], AF.Exp, scale=-0.5),
               inc=sems["p4rs"])
            xw = [(sems["p4sn"], 16 * (tc - 1))] if tc >= 2 else []
            do(p, "vector", lambda e, par=par, s=s: e.tensor_scalar(xn2b[par][:, :], x2b[:, :],
                                                                    s[:, 2:3], s[:, 5:6],
                                                                    ALU.add, ALU.mult),
               inc=sems["p4xn"], waits=xw + [(sems["p4rs"], sems["p4rs"].n)])
            do(p, "gpsimd", lambda e, tc=tc, par=par: e.dma_start(s_xn2[tc * 128:(tc + 1) * 128, :],
                                                                  xn2b[par][:, :]),
               inc=sems["p4sn"], inc_k=16, waits=[(sems["p4xn"], tc + 1)])
            if tc + 2 < NT:
                p4_load(tc + 2)

        run_phase(p)
        st4.close()
        st2.close()
        p4sn_total = sems["p4sn"].n
        if phases == 4:
            drain_all()
            return nc
        p4s2_total = sems["p4s2"].n

        st5 = ExitStack()
        en5 = st5.enter_context
        qT = en5(nc.sbuf_tensor("qT", [128, ND * D], BF16))
        xn2T = en5(nc.sbuf_tensor("xn2T", [128, ND * T], BF16))
        biasB = en5(nc.sbuf_tensor("biasB", [128, D], F32))
        ybuf = [en5(nc.sbuf_tensor(f"yb{i}", [128, D], F32)) for i in range(2)]
        x2rb = [en5(nc.sbuf_tensor(f"x2rb{i}", [128, D], F32)) for i in range(2)]
        scrap5 = en5(nc.sbuf_tensor("scrap5", [128, D], F32))
        stt5 = [en5(nc.sbuf_tensor(f"p5s{i}", [128, 8], F32)) for i in range(2)]
        ps5 = [en5(nc.psum_tensor(f"ps5_{i}", [128, D], F32)) for i in range(2)]

        p = new_prog()
        p4qt_total = sems["p4qt"].n
        p4xt_total = sems["p4xt"].n
        for ic in range(ND):
            do(p, "sync",
               lambda e, ic=ic: e.dma_start(qT[:, ic * D:(ic + 1) * D],
                                            s_qfT[ic * 128:(ic + 1) * 128, :]),
               inc=sems["p5tr"], inc_k=16,
               waits=[(sems["p4qt"], p4qt_total)] if ic == 0 else ())
        for ic in range(ND):
            do(p, "sync",
               lambda e, ic=ic: e.dma_start(xn2T[:, ic * T + TH:(ic + 1) * T],
                                            s_xn2[TH:T, ic * 128:(ic + 1) * 128], transpose=True),
               inc=sems["p5trs"], inc_k=16,
               waits=[(sems["p4sn"], p4sn_total)] if ic == 0 else ())
        for ic in range(ND):
            do(p, "gpsimd",
               lambda e, ic=ic: e.dma_start(xn2T[:, ic * T: ic * T + TH],
                                            s_xn2T[ic * 128:(ic + 1) * 128, 0:TH]),
               inc=sems["p5xs"], inc_k=16,
               waits=[(sems["p4xt"], p4xt_total)] if ic == 0 else ())
        do(p, "gpsimd", lambda e: e.dma_start(biasB[:, :], b_in.broadcast_to([128, D])),
           inc=sems["biasld"], inc_k=16)
        p5xl = [sems["p5xl0"], sems["p5xl1"]]
        for tc in range(NT):
            par = tc % 2
            w = [(sems["p4s2"], p4s2_total)] if tc == 0 else []
            if tc >= 2:
                w.append((sems["p5out"], 16 * (tc - 1)))
            do(p, "gpsimd",
               lambda e, tc=tc, par=par: e.dma_start(x2rb[par][:, :], s_x2[tc * 128:(tc + 1) * 128, :]),
               inc=p5xl[par], inc_k=16, waits=w)
        for tc in range(NT):
            par = tc % 2
            s = stt5[par]
            w = [(sems["p5tr"], 16 * ND), (sems["p5trs"], 16 * ND), (sems["p5xs"], 16 * ND)]
            if tc >= 2:
                w.append((sems["p5yb"], 2 * tc - 3))
            for ic in range(ND):
                for oq in range(nq):
                    last = (ic == ND - 1 and oq == nq - 1)
                    do(p, "tensor",
                       lambda e, tc=tc, par=par, ic=ic, oq=oq: e.matmul(
                           ps5[par][:, oq * pc:(oq + 1) * pc],
                           xn2T[:, ic * T + tc * 128: ic * T + (tc + 1) * 128],
                           qT[:, ic * D + oq * pc: ic * D + (oq + 1) * pc],
                           start=(ic == 0), stop=(ic == ND - 1)),
                       inc=(sems["p5mm"] if last else None),
                       waits=(w if (ic == 0 and oq == 0) else ()))
            yw = [(sems["p5mm"], tc + 1), (sems["biasld"], 16)]
            if tc >= 2:
                yw.append((sems["p5out"], 16 * (tc - 1)))
            vch = sems["vch"]
            do(p, "vector",
               lambda e, par=par, s=s: e.scalar_tensor_tensor(
                   ybuf[par][:, :], ps5[par][:, :], 1.0, biasB[:, :],
                   ALU.mult, ALU.add),
               inc=sems["p5yb"], waits=yw)
            do(p, "vector", lambda e, par=par, s=s: e.reduce_sum(s[:, 0:1], ybuf[par][:, :], axis=AXX),
               inc=vch, waits=[(sems["p5rs"], tc - 1)])
            do(p, "scalar", lambda e, par=par, s=s: e.activation(scrap5[:, :], ybuf[par][:, :], AF.Square,
                                                                accum_out=s[:, 1:2]),
               inc=sems["p5sq"], waits=[(sems["p5yb"], 2 * tc + 1), (sems["p5rs"], tc - 1)])
            do(p, "vector", lambda e, s=s: e.tensor_scalar(s[:, 2:3], s[:, 0:1], -1.0 / D, None, ALU.mult),
               waits=[(vch, vch.n)], inc=vch)
            do(p, "vector", lambda e, s=s: e.tensor_tensor(s[:, 3:4], s[:, 2:3], s[:, 2:3], ALU.mult),
               waits=[(vch, vch.n)], inc=vch)
            do(p, "vector", lambda e, s=s: e.tensor_scalar(s[:, 4:5], s[:, 1:2], 1.0 / D, s[:, 3:4],
                                                           ALU.mult, ALU.subtract),
               waits=[(vch, vch.n), (sems["p5sq"], sems["p5sq"].n)], inc=sems["p5var"])
            do(p, "scalar", lambda e, s=s: e.activation(s[:, 5:6], s[:, 4:5], AF.Ln, bias=np.float32(EPS)),
               waits=[(sems["p5var"], sems["p5var"].n)])
            do(p, "scalar", lambda e, s=s: e.activation(s[:, 5:6], s[:, 5:6], AF.Exp, scale=-0.5),
               inc=sems["p5rs"])
            do(p, "vector",
               lambda e, par=par, s=s: e.tensor_scalar(ybuf[par][:, :], ybuf[par][:, :],
                                                       s[:, 2:3], s[:, 5:6], ALU.add, ALU.mult),
               waits=[(sems["p5rs"], tc + 1)])
            do(p, "vector",
               lambda e, par=par: e.tensor_tensor(ybuf[par][:, :], ybuf[par][:, :], x2rb[par][:, :], ALU.add),
               inc=sems["p5yb"],
               waits=[(p5xl[par], 16 * (tc // 2 + 1))])
            do(p, "sync", lambda e, tc=tc, par=par: e.dma_start(out[tc * 128:(tc + 1) * 128, :],
                                                                ybuf[par][:, :]),
               inc=sems["p5out"], inc_k=16, waits=[(sems["p5yb"], sems["p5yb"].n)])
        waitonly(p, "sync", [(sems["p5out"], 16 * NT)])

        run_phase(p)
        st5.close()

    return nc


_NC_CACHE = {}


def get_nc(T=2048, D=2048, NH=16):
    key = (T, D, NH)
    if key not in _NC_CACHE:
        _NC_CACHE[key] = build_nc(T, D, NH)
    return _NC_CACHE[key]


def kernel(L0_attn_x, L0_attn_Wq, L0_attn_Wk, L0_attn_Wv, L0_attn_Wo,
           L0_ffn_weight, L0_ffn_bias):
    x = np.ascontiguousarray(np.asarray(L0_attn_x, dtype=np.float32))
    wq = np.ascontiguousarray(np.asarray(L0_attn_Wq, dtype=np.float32))
    wk = np.ascontiguousarray(np.asarray(L0_attn_Wk, dtype=np.float32))
    wv = np.ascontiguousarray(np.asarray(L0_attn_Wv, dtype=np.float32))
    wo = np.ascontiguousarray(np.asarray(L0_attn_Wo, dtype=np.float32))
    wf = np.ascontiguousarray(np.asarray(L0_ffn_weight, dtype=np.float32))
    fb = np.ascontiguousarray(np.asarray(L0_ffn_bias, dtype=np.float32)).reshape(1, -1)

    B = x.shape[0]
    nc = get_nc(x.shape[1], x.shape[2], 16)
    in_maps = [{"x": np.ascontiguousarray(x[b]), "wq": wq, "wk": wk, "wv": wv,
                "wo": wo, "wf": wf, "fbias": fb} for b in range(B)]
    res = run_bass_kernel_spmd(nc, in_maps, core_ids=list(range(B)))
    return np.stack([np.asarray(r["out"], dtype=np.float32) for r in res.results], axis=0)
